# version 10 (final)
# baseline (speedup 1.0000x reference)
import os
import numpy as np
from contextlib import ExitStack
import concourse.bass as bass
import concourse.mybir as mybir
from concourse.bass_utils import run_bass_kernel_spmd

F32 = mybir.dt.float32
BF16 = mybir.dt.bfloat16
FP8 = mybir.dt.float8e4
I32 = mybir.dt.int32
AF = mybir.ActivationFunctionType
ALU = mybir.AluOpType
AX = mybir.AxisListType.X

D = 1024
T = 4096
NT = T // 128
H = 16
DH = 64
INNER = 4096
R = 64
TOWN = 2048
NOWN = TOWN // 128
NEG_SHIFT = -15.0
MAGIC = 12582912.0
VW = 65


class SemPool:

    def __init__(self, nc, es, sizes):
        self.free = {}
        for kind, n in sizes.items():
            self.free[kind] = [es.enter_context(nc.semaphore(f"{kind}sem{i}"))
                               for i in range(n)][::-1]

    def acquire(self, kind):
        return self.free[kind].pop()

    def release(self, kind, s):
        self.free[kind].append(s)


_SEMPOOL = None


class P:

    def __init__(self, nc, es):
        self.nc, self.es = nc, es
        self._sems = []

    def sb(self, name, shape, dt):
        return self.es.enter_context(self.nc.sbuf_tensor(name, shape, dt))

    def ps(self, name, shape, dt):
        return self.es.enter_context(self.nc.psum_tensor(name, shape, dt))

    def sem(self, name, sw=False, act=False):
        kind = "sw" if sw else ("act" if act else "hw")
        s = _SEMPOOL.acquire(kind)
        if kind != "sw":
            self.es.callback(_SEMPOOL.release, kind, s)
        self._sems.append((kind, s))
        return s

    def clear_sems(self):
        sems = list(self._sems)
        if not sems:
            return
        by = {"hw": [], "sw": [], "act": []}
        for kind, s in sems:
            by[kind].append(s)
        with self.nc.Block() as blk:
            if by["hw"]:
                @blk.sync
                def _(e):
                    for s in by["hw"]:
                        e.sem_clear(s)
            if by["sw"]:
                @blk.gpsimd
                def _(e):
                    for s in by["sw"]:
                        e.sem_clear(s)
            if by["act"]:
                @blk.scalar
                def _(e):
                    for s in by["act"]:
                        e.sem_clear(s)


class RegWalk:

    def __init__(self, e, rg, src_ap):
        self.e, self.rg = e, rg
        e.reg_load(rg, src_ap)
        self.cur = 0

    def to(self, static):
        if static != self.cur:
            self.e.reg_add(self.rg, self.rg, static - self.cur)
            self.cur = static
        return self.rg


def build_program(taps=()):
    taps = set(taps)
    nc = bass.Bass("TRN2", target_bir_lowering=False, detect_race_conditions=False)

    def din(name, shape):
        return nc.dram_tensor(name, shape, F32, kind="ExternalInput").ap()

    x_in = din("x", [T, D])
    wq_in = din("Wq", [D, D]); wk_in = din("Wk", [D, D]); wv_in = din("Wv", [D, D])
    wout_in = din("Wout", [D, D])
    wg_in = din("w_gate", [INNER, D]); wu_in = din("w_up", [INNER, D]); wd_in = din("w_down", [D, INNER])
    wQ1_in = din("WQ1", [D, D]); wK1_in = din("WK1", [D, D]); wO1_in = din("WO1", [D, D])
    winv_in = din("Winv", [D, D])
    ub_in = din("U_b", [D, R]); vb_in = din("V_b", [D, R]); wb_in = din("W_b", [D, R])
    ut_in = din("U_t", [D, R]); vt_in = din("V_t", [D, R]); wt_in = din("W_t", [D, R])
    xt_in = din("X_t", [D, R])
    biasb_in = din("bias_b", [1, D]); biast_in = din("bias_t", [1, D])
    w1_in = din("W1", [INNER, D]); b1_in = din("b1", [128, INNER // 128])
    w2_in = din("W2", [D, INNER]); b2_in = din("b2", [1, D])
    scal_in = din("scal", [1, 6])
    hoff_in = nc.dram_tensor("hoffs", [1, 4], I32, kind="ExternalInput").ap()
    rcf_in = din("rcf", [1, T])
    rco_in = din("rco", [128, NOWN])
    ident_in = din("ident", [128, 128])

    y_out = nc.dram_tensor("y", [TOWN, D], F32, kind="ExternalOutput").ap()

    def scr(name, shape, dt):
        return nc.dram_tensor(name, shape, dt, kind="Internal").ap()

    xqT_sp = scr("xqT_sp", [NT, 128, 8 * 128], BF16)
    xnT_sp = scr("xnT_sp", [8, 128, T], BF16)
    q_sp = scr("q_sp", [8, 128, T], BF16)
    attn_sp = scr("attn_sp", [T, D], F32)
    x1_sp = scr("x1_sp", [T, D], F32)
    x1o_sp = scr("x1o_sp", [TOWN, D], F32)
    wgq_sp = scr("wgq_sp", [INNER, D], BF16)
    wuq_sp = scr("wuq_sp", [INNER, D], BF16)
    wdq_sp = scr("wdq_sp", [D, INNER], BF16)
    w1b_sp = scr("w1b_sp", [INNER, D], BF16)
    w2b_sp = scr("w2b_sp", [D, INNER], BF16)
    wob_sp = scr("wob_sp", [D, D], BF16)

    dbg = {}

    def tap(name, shape, dtype):
        if name in taps:
            dbg[name] = nc.dram_tensor("tap_" + name, shape, dtype, kind="ExternalOutput").ap()
            return dbg[name]
        return None

    es = ExitStack()
    with es:
        global _SEMPOOL
        _SEMPOOL = SemPool(nc, es, {"sw": 34, "act": 3, "hw": 44})
        g = P(nc, es)
        ident_bf = g.sb("ident_bf", [128, 128], BF16)
        ident_f = g.sb("ident_f", [128, 128], F32)
        consts = g.sb("consts", [128, 8], F32)
        stat_m = g.sb("stat_m", [128, NT], F32)
        stat_r = g.sb("stat_r", [128, NT], F32)
        stat_m1 = g.sb("stat_m1", [128, NT], F32)
        stat_r1 = g.sb("stat_r1", [128, NT], F32)
        invsx = g.sb("invsx", [128, NT], F32)
        invsa = g.sb("invsa", [128, NT], F32)
        scal_sb = g.sb("scal_sb", [1, 6], F32)
        hoff_sb = g.sb("hoff_sb", [1, 4], I32)
        rco_sb = g.sb("rco_sb", [128, NOWN], F32)
        swr = g.sb("swr", [128, 4], F32)
        onerow_f = g.sb("onerow_f", [1, 128], F32)
        ld0 = g.sem("ld0")
        ld0g = g.sem("ld0g", sw=True)

        with nc.Block() as blk:
            @blk.sync
            def _(e):
                e.dma_start(out=ident_f[:], in_=ident_in[:, :]).then_inc(ld0, 16)
                e.dma_start(out=scal_sb[:], in_=scal_in[:, :]).then_inc(ld0, 16)
                e.dma_start(out=hoff_sb[:], in_=hoff_in[:, :]).then_inc(ld0, 16)
                e.dma_start(out=rco_sb[:], in_=rco_in[:, :]).then_inc(ld0, 16)

            @blk.gpsimd
            def _(e):
                e.dma_start(out=ident_bf[:], in_=ident_in[:, :]).then_inc(ld0g, 16)
                e.wait_ge(ld0, 64)
                e.wait_ge(ld0g, 16)
                e.memset(consts[:, 0:1], NEG_SHIFT)
                e.memset(consts[:, 1:2], 0.125)
                e.memset(consts[:, 2:3], 1.0)
                e.memset(onerow_f[:], 1.0)

        with nc.named_scope("P1"), ExitStack() as es_xn:
            xn0T = es_xn.enter_context(nc.sbuf_tensor("xn0T", [128, 8 * T], BF16))
            _p1(nc, g, x_in, xqT_sp, xn0T, stat_m, stat_r, invsx, ident_bf, taps, tap)
            sxn = g.sem("sxn_spill")
            with nc.Block() as blk:
                @blk.sync
                def _(e):
                    e.sem_clear(sxn)
                    for c in range(8):
                        e.dma_start(out=xnT_sp[c, :, :], in_=xn0T[:, T * c:T * (c + 1)]).then_inc(sxn, 16)
                    e.wait_ge(sxn, 128)

        with nc.named_scope("PWQ"):
            _pw_quant(nc, g, wg_in, wgq_sp, INNER, D, swr, 0, ident_f, tap, "wsg")
            _pw_quant(nc, g, wu_in, wuq_sp, INNER, D, swr, 1, ident_f, tap, "wsu")
            _pw_quant(nc, g, wd_in, wdq_sp, D, INNER, swr, 2, ident_f, tap, "wsd")

        with ExitStack() as es_kv:
            K_sb = es_kv.enter_context(nc.sbuf_tensor("K_sb", [128, 8 * T], BF16))
            V_sb = es_kv.enter_context(nc.sbuf_tensor("V_sb", [128, NT * H * VW], BF16))
            with nc.named_scope("P2"):
                _p2(nc, g, xnT_sp, K_sb, V_sb, wq_in, wk_in, wv_in, q_sp, tap)
            with nc.named_scope("P3"):
                _p3(nc, g, K_sb, V_sb, q_sp, wout_in, attn_sp, consts, tap)

        if os.environ.get("KSTOP") == "p3":
            return nc, dbg
        with nc.named_scope("P4"):
            _p4(nc, g, xqT_sp, attn_sp, x_in, x1_sp, wgq_sp, wuq_sp, wdq_sp,
                invsx, invsa, swr, stat_m1, stat_r1, ident_bf, tap)

        if os.environ.get("KSTOP") == "p4":
            return nc, dbg
        xn1T_own = g.sb("xn1T_own", [128, 8 * TOWN], BF16)
        rcatq_own = g.sb("rcatq_own", [128, TOWN], BF16)
        tpTb = g.sb("tpTb", [4, TOWN], BF16)
        lt_sb = g.sb("lt_sb", [16, D], BF16)
        M_sb = g.sb("M_sb", [128, 32 * 16], BF16)
        E2b = g.sb("E2b", [16, D], BF16)
        oddT = g.sb("oddT", [128, 64], BF16)
        bc = g.sb("bc", [128, 8], F32)

        with nc.named_scope("P5"):
            _p5(nc, g, x1_sp, stat_m1, stat_r1, xn1T_own, rcatq_own, tpTb,
                lt_sb, M_sb, E2b, oddT, bc,
                wQ1_in, wK1_in, winv_in, vb_in, wb_in, vt_in, wt_in, xt_in, rcf_in,
                scal_sb, ident_bf, ident_f, onerow_f, tap)

        if os.environ.get("KSTOP") == "p5":
            return nc, dbg
        tpTr = g.sb("tpTr", [1, 4 * TOWN], BF16)
        ev1r = g.sb("ev1r", [1, INNER], BF16)
        srow = g.sem("srow")
        with nc.Block() as blk:
            @blk.sync
            def _(e):
                e.sem_clear(srow)
                for q in range(4):
                    e.dma_start(out=tpTr[0:1, TOWN * q:TOWN * (q + 1)],
                                in_=tpTb[q:q + 1, :]).then_inc(srow, 16)
                for q in range(4):
                    e.dma_start(out=ev1r[0:1, D * q:D * (q + 1)],
                                in_=lt_sb[2 * q:2 * q + 1, :]).then_inc(srow, 16)
                e.wait_ge(srow, 128)
        tp_tpTr = tap("tpTr", [1, 4 * TOWN], BF16)
        tp_ev1r = tap("ev1r", [1, INNER], BF16)
        if tp_tpTr is not None or tp_ev1r is not None:
            srt = g.sem("srow_tap")
            with nc.Block() as blk:
                @blk.sync
                def _(e):
                    e.sem_clear(srt)
                    n = 0
                    if tp_tpTr is not None:
                        e.dma_start(out=tp_tpTr[:, :], in_=tpTr[:]).then_inc(srt, 16); n += 16
                    if tp_ev1r is not None:
                        e.dma_start(out=tp_ev1r[:, :], in_=ev1r[:]).then_inc(srt, 16); n += 16
                    e.wait_ge(srt, n)

        if os.environ.get("KSTOP") == "row":
            return nc, dbg
        with nc.named_scope("PWB"):
            _pw_bf16(nc, g, w1_in, w1b_sp, INNER, D, "w1")
            _pw_bf16(nc, g, w2_in, w2b_sp, D, INNER, "w2")
            _pw_bf16(nc, g, wO1_in, wob_sp, D, D, "wo")

        if os.environ.get("KSTOP") == "pw":
            return nc, dbg
        with nc.named_scope("P7"):
            _p7(nc, g, rcatq_own, x1_sp, x1o_sp, wob_sp, ub_in, ut_in, biasb_in,
                biast_in, bc, rco_sb, onerow_f, ident_bf, tap)

        if os.environ.get("KSTOP") == "p7":
            return nc, dbg
        with nc.named_scope("P8"):
            _p8(nc, g, xn1T_own, x1o_sp, w1b_sp, w2b_sp, tpTr, ev1r, M_sb, E2b,
                b1_in, b2_in, y_out, onerow_f, tap)

        tp_xn0T = tap("xn0T", [128, 8 * T], BF16)
        tp_invsx = tap("invsx", [128, NT], F32)
        tp_statr = tap("statr", [128, NT], F32)
        tp_statm = tap("statm", [128, NT], F32)
        if tp_xn0T is not None or tp_invsx is not None:
            fin = g.sem("fin_tap")
            with nc.Block() as blk:
                @blk.sync
                def _(e):
                    e.sem_clear(fin)
                    n = 0
                    if tp_xn0T is not None:
                        e.dma_start(out=tp_xn0T[:].rearrange("p (c t) -> c p t", c=8),
                                    in_=xnT_sp[:, :, :]).then_inc(fin, 16); n += 16
                    if tp_invsx is not None:
                        e.dma_start(out=tp_invsx[:, :], in_=invsx[:]).then_inc(fin, 16); n += 16
                    if tp_statr is not None:
                        e.dma_start(out=tp_statr[:, :], in_=stat_r[:]).then_inc(fin, 16); n += 16
                    if tp_statm is not None:
                        e.dma_start(out=tp_statm[:, :], in_=stat_m[:]).then_inc(fin, 16); n += 16
                    e.wait_ge(fin, n)


    return nc, dbg


def _p1(nc, g, x_in, xqT_sp, xn0T, stat_m, stat_r, invsx, ident_bf, taps, tap):
    tp_xqT = tap("xqT", [NT, 128, 8 * 128], BF16)
    with ExitStack() as es:
        p = P(nc, es)
        xin = [p.sb(f"p1_xin{i}", [128, D], F32) for i in range(2)]
        junk = [p.sb(f"p1_junk{i}", [128, D], F32) for i in range(2)]
        scol = p.sb("p1_scol", [128, NT], F32)
        qcol = p.sb("p1_qcol", [128, NT], F32)
        vcol = p.sb("p1_vcol", [128, NT], F32)
        tcol = p.sb("p1_tcol", [128, NT], F32)
        sx = [p.sem("p1_sx0"), p.sem("p1_sx1")]
        sn = p.sem("p1_sn"); sv = p.sem("p1_sv")
        p.clear_sems()

        with nc.Block() as blk:
            @blk.sync
            def _(e):
                for t in range(NT):
                    if t >= 2:
                        e.wait_ge(sn, t - 1)
                        e.wait_ge(sv, t - 1)
                    e.dma_start(out=xin[t % 2][:], in_=x_in[128 * t:128 * (t + 1), :]).then_inc(sx[t % 2], 16)

            @blk.scalar
            def _(e):
                for t in range(NT):
                    e.wait_ge(sx[t % 2], 16 * (t // 2 + 1))
                    if t >= 2:
                        e.wait_ge(sv, t - 1)
                    e.activation(out=junk[t % 2][:], in_=xin[t % 2][:],
                                 func=AF.Square).then_inc(sn, 1)
                e.wait_ge(sv, NT + 1)
                e.activation(out=stat_r[:], in_=vcol[:], func=AF.Sqrt).then_inc(sn, 1)

            @blk.vector
            def _(e):
                for t in range(NT):
                    e.wait_ge(sx[t % 2], 16 * (t // 2 + 1))
                    e.reduce_sum(scol[:, t:t + 1], xin[t % 2][:], axis=AX)
                    e.wait_ge(sn, t + 1)
                    e.reduce_sum(qcol[:, t:t + 1], junk[t % 2][:], axis=AX).then_inc(sv, 1)
                e.drain()
                e.tensor_single_scalar(out=scol[:], in_=scol[:], scalar=1.0 / D, op=ALU.mult)
                e.tensor_single_scalar(out=qcol[:], in_=qcol[:], scalar=1.0 / D, op=ALU.mult)
                e.drain()
                e.tensor_tensor(out=tcol[:], in0=scol[:], in1=scol[:], op=ALU.mult)
                e.drain()
                e.tensor_sub(tcol[:], qcol[:], tcol[:])
                e.drain()
                e.tensor_single_scalar(out=tcol[:], in_=tcol[:], scalar=1e-5, op=ALU.add)
                e.drain()
                e.reciprocal(vcol[:], tcol[:]).then_inc(sv, 1)
                e.wait_ge(sn, NT + 1)
                for _ in range(2):
                    e.tensor_tensor(out=qcol[:], in0=stat_r[:], in1=stat_r[:], op=ALU.mult)
                    e.drain()
                    e.tensor_tensor(out=qcol[:], in0=qcol[:], in1=tcol[:], op=ALU.mult)
                    e.drain()
                    e.tensor_scalar(out=qcol[:], in0=qcol[:], scalar1=-0.5, scalar2=1.5,
                                    op0=ALU.mult, op1=ALU.add)
                    e.drain()
                    e.tensor_tensor(out=stat_r[:], in0=stat_r[:], in1=qcol[:], op=ALU.mult)
                    e.drain()
                e.tensor_tensor(out=stat_m[:], in0=scol[:], in1=stat_r[:], op=ALU.mult)
                e.drain()
                e.tensor_single_scalar(out=stat_m[:], in_=stat_m[:], scalar=-1.0, op=ALU.mult)

        _p1b(nc, g, x_in, xqT_sp, xn0T, stat_m, stat_r, invsx, ident_bf, tp_xqT)


def _p1b(nc, g, x_in, xqT_sp, xn0T, stat_m, stat_r, invsx, ident_bf, tp_xqT):
    with ExitStack() as es:
        p = P(nc, es)
        xin = [p.sb(f"p1b_xin{i}", [128, D], F32) for i in range(2)]
        xn_f = p.sb("p1b_xnf", [128, D], F32)
        xn_bf = [p.sb(f"p1b_xnbf{i}", [128, D], BF16) for i in range(2)]
        xq_i = [p.sb(f"p1b_xqi{i}", [128, D], I32) for i in range(2)]
        xq_bf = [p.sb(f"p1b_xqbf{i}", [128, D], BF16) for i in range(2)]
        xqT_t = [p.sb(f"p1b_xqTt{i}", [128, 8 * 128], BF16) for i in range(2)]
        junk2 = p.sb("p1b_junk2", [128, D], F32)
        st1 = p.sb("p1b_st", [128, 4], F32)
        ptr = [p.ps(f"p1b_ptr{i}", [128, 512], BF16) for i in range(4)]
        sx = [p.sem("p1b_sx0"), p.sem("p1b_sx1")]
        sv = p.sem("p1b_sv"); sg = p.sem("p1b_sg", sw=True)
        str_ = p.sem("p1b_str"); sev = p.sem("p1b_sev")
        ssp = [p.sem("p1b_ssp0"), p.sem("p1b_ssp1")]
        p.clear_sems()

        with nc.Block() as blk:
            @blk.sync
            def _(e):
                for t in range(NT):
                    if t >= 2:
                        e.wait_ge(sv, 2 * (t - 2) + 2)
                    e.dma_start(out=xin[t % 2][:], in_=x_in[128 * t:128 * (t + 1), :]).then_inc(sx[t % 2], 16)
                    if t >= 1:
                        e.wait_ge(sev, 4 * t)
                        e.dma_start(out=xqT_sp[t - 1, :, :], in_=xqT_t[(t - 1) % 2][:]).then_inc(ssp[(t - 1) % 2], 16)
                e.wait_ge(sev, 4 * NT)
                e.dma_start(out=xqT_sp[NT - 1, :, :], in_=xqT_t[(NT - 1) % 2][:]).then_inc(ssp[(NT - 1) % 2], 16)
                e.wait_ge(ssp[0], 16 * (NT // 2))
                e.wait_ge(ssp[1], 16 * (NT // 2))
                if tp_xqT is not None:
                    e.dma_start(out=tp_xqT[:, :, :], in_=xqT_sp[:, :, :]).then_inc(ssp[0], 16)
                    e.wait_ge(ssp[0], 16 * (NT // 2) + 16)

            @blk.vector
            def _(e):
                for t in range(NT):
                    e.wait_ge(sx[t % 2], 16 * (t // 2 + 1))
                    e.tensor_scalar(out=xn_f[:], in0=xin[t % 2][:],
                                    scalar1=stat_r[:, t:t + 1], scalar2=stat_m[:, t:t + 1],
                                    op0=ALU.mult, op1=ALU.add).then_inc(sv, 1)
                    e.drain()
                    e.reduce_max(st1[:, 0:1], xn_f[:], axis=AX, apply_absolute_value=True)
                    e.drain()
                    e.tensor_single_scalar(out=st1[:, 0:1], in_=st1[:, 0:1], scalar=1e-5, op=ALU.max)
                    e.drain()
                    e.reciprocal(st1[:, 1:2], st1[:, 0:1])
                    e.drain()
                    e.tensor_single_scalar(out=st1[:, 1:2], in_=st1[:, 1:2], scalar=127.0, op=ALU.mult)
                    e.tensor_single_scalar(out=invsx[:, t:t + 1], in_=st1[:, 0:1], scalar=1.0 / 127.0, op=ALU.mult)
                    if t >= 2:
                        e.wait_ge(str_, 4 * (t - 2) + 4)
                    e.drain()
                    e.tensor_scalar(out=junk2[:], in0=xn_f[:],
                                    scalar1=st1[:, 1:2], scalar2=MAGIC,
                                    op0=ALU.mult, op1=ALU.add)
                    e.drain()
                    e.tensor_single_scalar(out=xq_i[t % 2][:], in_=junk2[:],
                                           scalar=-MAGIC, op=ALU.add)
                    e.tensor_copy(xn_bf[t % 2][:], xn_f[:]).then_inc(sv, 1)
                    if t >= 1:
                        _p1_evac(e, t - 1, ptr, xn0T, xqT_t, str_, sev, ssp)
                _p1_evac(e, NT - 1, ptr, xn0T, xqT_t, str_, sev, ssp)

            @blk.gpsimd
            def _(e):
                for t in range(NT):
                    e.wait_ge(sv, 2 * t + 2)
                    e.tensor_scalar(out=xq_bf[t % 2][:], in0=xq_i[t % 2][:],
                                    scalar1=127, scalar2=-128,
                                    op0=ALU.min, op1=ALU.max).then_inc(sg, 1)

            @blk.tensor
            def _(e):
                for t in range(NT):
                    e.wait_ge(sv, 2 * t + 2)
                    e.wait_ge(sg, t + 1)
                    if t >= 1:
                        e.wait_ge(sev, 4 * (t - 1) + 4)
                    for grp in range(2):
                        for j in range(4):
                            c = 4 * grp + j
                            ins = e.transpose(ptr[grp][:, 128 * j:128 * (j + 1)],
                                              xn_bf[t % 2][:, 128 * c:128 * (c + 1)], ident_bf[:])
                            if j == 3:
                                ins.then_inc(str_, 1)
                    for grp in range(2):
                        for j in range(4):
                            c = 4 * grp + j
                            ins = e.transpose(ptr[2 + grp][:, 128 * j:128 * (j + 1)],
                                              xq_bf[t % 2][:, 128 * c:128 * (c + 1)], ident_bf[:])
                            if j == 3:
                                ins.then_inc(str_, 1)


def _p1_evac(e, t, ptr, xn0T, xqT_t, str_, sev, ssp):
    for grp in range(2):
        e.wait_ge(str_, 4 * t + grp + 1)
        dst = xn0T[:].rearrange("p (c n) -> p c n", c=8)[:, 4 * grp:4 * grp + 4, 128 * t:128 * (t + 1)]
        e.tensor_copy(dst, ptr[grp][:].rearrange("p (j n) -> p j n", j=4)).then_inc(sev, 1)
    if t >= 2:
        e.wait_ge(ssp[t % 2], 16 * ((t - 2) // 2 + 1))
    for grp in range(2):
        e.wait_ge(str_, 4 * t + 2 + grp + 1)
        dst = xqT_t[t % 2][:, 512 * grp:512 * (grp + 1)]
        e.tensor_copy(dst, ptr[2 + grp][:]).then_inc(sev, 1)


def _p2(nc, g, xnT_sp, K_sb, V_sb, wq_in, wk_in, wv_in, q_sp, tap):
    tp_K = tap("K", [128, 8 * T], BF16)
    tp_V = tap("V", [128, NT * H * VW], BF16)
    tp_Q = tap("Q", [8, 128, T], BF16)

    with ExitStack() as es:
        p = P(nc, es)
        wk_bf = p.sb("p2_wk", [128, 8 * D], BF16)
        wq_bf = p.sb("p2_wq", [128, 8 * D], BF16)
        wv_bf = p.sb("p2_wv", [128, 8 * D], BF16)
        xblk = [p.sb(f"p2_xb{i}", [128, 8 * 512], BF16) for i in range(2)]
        qbuf = [p.sb(f"p2_qb{i}", [128, 512], BF16) for i in range(4)]
        pb = [p.ps(f"p2_pb{i}", [128, 512], F32) for i in range(4)]
        swl = p.sem("p2_swl", sw=True)
        sxb = [p.sem("p2_sxb0"), p.sem("p2_sxb1")]
        smm = p.sem("p2_smm"); sev = p.sem("p2_sev"); sones = p.sem("p2_sones", sw=True)
        sqs = [p.sem(f"p2_sqs{i}", sw=True) for i in range(4)]
        p.clear_sems()

        UPB = 8 + 8 + 8

        with nc.Block() as blk:
            @blk.gpsimd
            def _(e):
                for name, (wbf, win) in {"k": (wk_bf, wk_in), "q": (wq_bf, wq_in),
                                         "v": (wv_bf, wv_in)}.items():
                    for k in range(8):
                        e.dma_start(out=wbf[:, D * k:D * (k + 1)],
                                    in_=win[128 * k:128 * (k + 1), :]).then_inc(swl, 16)
                ap_ones = V_sb[:].rearrange("p (t h e) -> p t h e", t=NT, h=H)[:, :, :, 64:65]
                e.memset(ap_ones, 1.0).then_inc(sones, 1)
                for n in range(8):
                    for m in range(8):
                        u = UPB * n + 8 + m
                        qi = 8 * n + m
                        e.wait_ge(sev, u + 1)
                        e.dma_start(out=q_sp[m, :, 512 * n:512 * (n + 1)],
                                    in_=qbuf[qi % 4][:]).then_inc(sqs[qi % 4], 16)
                for i in range(4):
                    e.wait_ge(sqs[i], 16 * 16)

            @blk.sync
            def _(e):
                for n in range(8):
                    if n >= 2:
                        e.wait_ge(smm, UPB * (n - 1))
                    for c in range(8):
                        e.dma_start(out=xblk[n % 2][:, 512 * c:512 * (c + 1)],
                                    in_=xnT_sp[c, :, 512 * n:512 * (n + 1)]).then_inc(sxb[n % 2], 16)

            @blk.tensor
            def _(e):
                e.wait_ge(swl, 16 * 24)
                u = 0
                for n in range(8):
                    e.wait_ge(sxb[n % 2], 16 * 8 * (n // 2 + 1))
                    for wbf in (wk_bf, wq_bf):
                        for m in range(8):
                            if u >= 4:
                                e.wait_ge(sev, u - 3)
                            for k in range(8):
                                e.matmul(pb[u % 4][:],
                                         wbf[:, D * k + 128 * m:D * k + 128 * (m + 1)],
                                         xblk[n % 2][:, 512 * k:512 * (k + 1)],
                                         start=(k == 0), stop=(k == 7)).then_inc(smm, 1 if k == 7 else 0)
                            u += 1
                    for tt in range(4):
                        for half in range(2):
                            if u >= 4:
                                e.wait_ge(sev, u - 3)
                            for k in range(8):
                                e.matmul(pb[u % 4][:],
                                         xblk[n % 2][:, 512 * k + 128 * tt:512 * k + 128 * (tt + 1)],
                                         wv_bf[:, D * k + 512 * half:D * k + 512 * (half + 1)],
                                         start=(k == 0), stop=(k == 7)).then_inc(smm, 1 if k == 7 else 0)
                            u += 1

            @blk.vector
            def _(e):
                e.wait_ge(sones, 1)
                u = 0
                for n in range(8):
                    for m in range(8):
                        e.wait_ge(smm, u + 1)
                        e.tensor_copy(K_sb[:, T * m + 512 * n:T * m + 512 * (n + 1)],
                                      pb[u % 4][:]).then_inc(sev, 1)
                        u += 1
                    for m in range(8):
                        e.wait_ge(smm, u + 1)
                        qi = 8 * n + m
                        if qi >= 4:
                            e.wait_ge(sqs[qi % 4], 16 * ((qi - 4) // 4 + 1))
                        e.tensor_copy(qbuf[qi % 4][:], pb[u % 4][:]).then_inc(sev, 1)
                        u += 1
                    for tt in range(4):
                        t = 4 * n + tt
                        for half in range(2):
                            e.wait_ge(smm, u + 1)
                            dst = V_sb[:].rearrange("p (t h e) -> p t h e", t=NT, h=H)[
                                :, t, 8 * half:8 * (half + 1), 0:64]
                            e.tensor_copy(dst, pb[u % 4][:].rearrange("p (h e) -> p h e", h=8)).then_inc(sev, 1)
                            u += 1

        if tp_K is not None or tp_V is not None or tp_Q is not None:
            sv2 = g.sem("p2_tap")
            with nc.Block() as blk:
                @blk.sync
                def _(e):
                    e.sem_clear(sv2)
                    nw = 0
                    if tp_K is not None:
                        e.dma_start(out=tp_K[:, :], in_=K_sb[:]).then_inc(sv2, 16); nw += 16
                    if tp_V is not None:
                        e.dma_start(out=tp_V[:, :], in_=V_sb[:]).then_inc(sv2, 16); nw += 16
                    if tp_Q is not None:
                        e.dma_start(out=tp_Q[:, :, :], in_=q_sp[:, :, :]).then_inc(sv2, 16); nw += 16
                    e.wait_ge(sv2, nw)


def _p3(nc, g, K_sb, V_sb, q_sp, wout_in, attn_sp, consts, tap):
    NQ3 = 8
    tp_attn = tap("attn", [T, D], F32)
    tp_ofm = tap("ofm", [128, 8 * 512], BF16)
    tp_rep = tap("rep", [64, 512], F32)
    SWEEPS = [list(range(0, 4)), list(range(4, 8)), list(range(8, 12)), list(range(12, 16))]

    units = []
    quad_first_unit = []
    sweep_bounds = []
    for jq in range(NQ3):
        quad_first_unit.append(len(units))
        for si, heads in enumerate(SWEEPS):
            u0 = len(units)
            for i in range(4 * jq + 4):
                for h in heads:
                    units.append((jq, si, i, h, i >= 4 * jq))
            sweep_bounds.append((jq, si, u0, len(units)))
    NU = len(units)
    quad_first_unit.append(NU)
    masked_prefix = [0]
    for (_jq, _si, _i, _h, _m) in units:
        masked_prefix.append(masked_prefix[-1] + (1 if _m else 0))
    evac_cum = {}
    tot = 0
    for (jq, si, u0, u1) in sweep_bounds:
        tot += len(SWEEPS[si])
        evac_cum[(jq, si)] = tot

    def av_wait(e, uidx, sexp, smk):
        (jq, si, i, h, m) = units[uidx]
        if m:
            e.wait_ge(smk, masked_prefix[uidx + 1])
        else:
            e.wait_ge(sexp, uidx + 1)

    def av_mm(e, uidx, po, expb, sav):
        (jq, si, i, h, m) = units[uidx]
        slot = SWEEPS[si].index(h)
        e.matmul(po[slot][0:VW, :],
                 V_sb[:, VW * (H * i + h):VW * (H * i + h + 1)],
                 expb[uidx % 4][:],
                 start=(i == 0), stop=(i == 4 * jq + 3),
                 skip_group_check=True).then_inc(sav, 1)

    with ExitStack() as es:
        p = P(nc, es)
        wout_bf = p.sb("p3_wout", [128, 8 * D], BF16)
        qq = [p.sb(f"p3_qq{i}", [128, 8 * 512], BF16) for i in range(2)]
        expb = [p.sb(f"p3_exp{i}", [128, 512], BF16) for i in range(4)]
        o_fm = p.sb("p3_ofm", [128, 8 * 512], BF16)
        rcp = p.sb("p3_rcp", [1, 512], F32)
        rep_sb = p.sb("p3_rep", [128, 512], F32)
        ones_row = p.sb("p3_ones", [1, 64], F32)
        astage = [p.sb(f"p3_ast{i}", [128, D], F32) for i in range(2)]
        po = [p.ps(f"p3_po{i}", [128, 512], F32) for i in range(4)]
        prep = p.ps("p3_prep", [64, 512], F32)
        psc = [p.ps(f"p3_psc{i}", [128, 512], F32) for i in range(3)]
        swl = p.sem("p3_swl", sw=True)
        sor = p.sem("p3_sor", sw=True)
        sq = [p.sem("p3_sq0"), p.sem("p3_sq1")]
        sms = p.sem("p3_sms")
        sexp = p.sem("p3_sexp")
        smk = p.sem("p3_smk", sw=True)
        sav = p.sem("p3_sav")
        srp = p.sem("p3_srp")
        smr = p.sem("p3_smr")
        srb = p.sem("p3_srb")
        sevo = p.sem("p3_sevo")
        sma = p.sem("p3_sma")
        seva = p.sem("p3_seva")
        ssa = [p.sem("p3_ssa0", sw=True), p.sem("p3_ssa1", sw=True)]
        stp = p.sem("p3_stp")
        p.clear_sems()

        with nc.Block() as blk:
            @blk.gpsimd
            def _(e):
                for k in range(8):
                    e.dma_start(out=wout_bf[:, D * k:D * (k + 1)],
                                in_=wout_in[128 * k:128 * (k + 1), :]).then_inc(swl, 16)
                e.memset(ones_row[:], 1.0).then_inc(sor, 1)
                for u, (jq, si, i, h, m) in enumerate(units):
                    if m:
                        ko = i - 4 * jq
                        e.wait_ge(sexp, u + 1)
                        e.affine_select(out=expb[u % 4][:], in_=expb[u % 4][:],
                                        pattern=[[1, 512]], compare_op=ALU.is_ge,
                                        fill=0.0, base=-128 * ko, channel_multiplier=-1
                                        ).then_inc(smk, 1)
                    if u + 1 == quad_first_unit[jq + 1]:
                        for tt in range(4):
                            a = 8 * jq + 2 * tt
                            e.wait_ge(seva, a + 2)
                            e.dma_start(out=attn_sp[512 * jq + 128 * tt:512 * jq + 128 * (tt + 1), :],
                                        in_=astage[tt % 2][:]).then_inc(ssa[tt % 2], 16)
                e.wait_ge(ssa[0], 16 * 2 * NQ3)
                e.wait_ge(ssa[1], 16 * 2 * NQ3)

            @blk.sync
            def _(e):
                for jq in range(NQ3):
                    if jq >= 2:
                        e.wait_ge(sms, quad_first_unit[jq - 1])
                    for c in range(8):
                        e.dma_start(out=qq[jq % 2][:, 512 * c:512 * (c + 1)],
                                    in_=q_sp[c, :, 512 * jq:512 * (jq + 1)]).then_inc(sq[jq % 2], 16)
                ntap = 0
                if tp_attn is not None:
                    e.wait_ge(ssa[0], 16 * 2 * NQ3); e.wait_ge(ssa[1], 16 * 2 * NQ3)
                    e.dma_start(out=tp_attn[:, :], in_=attn_sp[:, :]).then_inc(stp, 16); ntap += 16
                if tp_ofm is not None:
                    e.wait_ge(sevo, 16 * NQ3)
                    e.dma_start(out=tp_ofm[:, :], in_=o_fm[:]).then_inc(stp, 16); ntap += 16
                if tp_rep is not None:
                    e.wait_ge(srb, 16 * NQ3)
                    e.dma_start(out=tp_rep[:, :], in_=rep_sb[0:64, :]).then_inc(stp, 16); ntap += 16
                if ntap:
                    e.wait_ge(stp, ntap)

            @blk.scalar
            def _(e):
                for u in range(NU):
                    e.wait_ge(sms, u + 1)
                    if u >= 4:
                        e.wait_ge(sav, u - 3)
                    e.activation(out=expb[u % 4][:], in_=psc[u % 3][:], func=AF.Exp,
                                 bias=consts[:, 0:1], scale=consts[:, 1:2]).then_inc(sexp, 1)

            @blk.tensor
            def _(e):
                e.wait_ge(swl, 16 * 8)
                e.wait_ge(sor, 1)
                u = 0
                rep_done = 0
                for jq in range(NQ3):
                    e.wait_ge(sq[jq % 2], 16 * 8 * (jq // 2 + 1))
                    e.wait_ge(seva, 8 * jq)
                    n_i = 4 * jq + 4
                    for si, heads in enumerate(SWEEPS):
                        u0 = u
                        prev = evac_cum[(jq, si - 1)] if si > 0 else (
                            evac_cum[(jq - 1, len(SWEEPS) - 1)] if jq > 0 else 0)
                        def issue_av(x):
                            av_wait(e, x, sexp, smk)
                            if x == u0:
                                e.wait_ge(sevo, prev)
                            av_mm(e, x, po, expb, sav)

                        for i in range(n_i):
                            for h in heads:
                                if u >= 3:
                                    e.wait_ge(sexp, u - 2)
                                e.matmul(psc[u % 3][:],
                                         K_sb[64 * (h % 2):64 * (h % 2) + 64,
                                              T * (h // 2) + 128 * i:T * (h // 2) + 128 * (i + 1)],
                                         qq[jq % 2][64 * (h % 2):64 * (h % 2) + 64,
                                                    512 * (h // 2):512 * (h // 2 + 1)],
                                         start=True, stop=True, skip_group_check=True,
                                         tile_position=(64 * (h % 2), 0)).then_inc(sms, 1)
                                if u - 2 >= u0:
                                    issue_av(u - 2)
                                u += 1
                        for x in (u - 2, u - 1):
                            if x >= u0:
                                issue_av(x)
                        for hx, h in enumerate(heads):
                            rep_done += 1
                            e.wait_ge(srp, rep_done)
                            if rep_done >= 2:
                                e.wait_ge(srb, rep_done - 1)
                            e.matmul(prep[0:64, :], ones_row[:], rcp[:],
                                     start=True, stop=True, skip_group_check=True).then_inc(smr, 1)
                    e.wait_ge(sevo, evac_cum[(jq, len(SWEEPS) - 1)])
                    for tt in range(4):
                        for nf in range(2):
                            a = 8 * jq + 2 * tt + nf
                            if a >= 2:
                                e.wait_ge(seva, a - 1)
                            for c in range(8):
                                e.matmul(psc[a % 2][:],
                                         o_fm[:, 512 * c + 128 * tt:512 * c + 128 * (tt + 1)],
                                         wout_bf[:, D * c + 512 * nf:D * c + 512 * (nf + 1)],
                                         start=(c == 0), stop=(c == 7),
                                         skip_group_check=True).then_inc(sma, 1 if c == 7 else 0)

            @blk.vector
            def _(e):
                av_total = 0
                rep_cnt = 0
                for jq in range(NQ3):
                    n_i = 4 * jq + 4
                    for si, heads in enumerate(SWEEPS):
                        av_total += n_i * len(heads)
                        for hx, h in enumerate(heads):
                            slot = hx
                            e.wait_ge(sav, av_total)
                            if jq > 0 or si > 0:
                                e.wait_ge(sma, 8 * jq)
                            e.reciprocal(rcp[:], po[slot][64:65, :]).then_inc(srp, 1)
                            rep_cnt += 1
                            e.wait_ge(smr, rep_cnt)
                            e.tensor_copy(rep_sb[0:64, :], prep[0:64, :]).then_inc(srb, 1)
                            e.drain()
                            dst = o_fm[64 * (h % 2):64 * (h % 2) + 64,
                                       512 * (h // 2):512 * (h // 2 + 1)]
                            e.tensor_tensor(out=dst, in0=po[slot][0:VW - 1, :],
                                            in1=rep_sb[0:64, :], op=ALU.mult).then_inc(sevo, 1)
                    for tt in range(4):
                        for nf in range(2):
                            a = 8 * jq + 2 * tt + nf
                            g_ = 4 * jq + tt
                            e.wait_ge(sma, a + 1)
                            if g_ >= 2 and nf == 0:
                                e.wait_ge(ssa[g_ % 2], 16 * (g_ // 2))
                            e.tensor_copy(astage[tt % 2][:, 512 * nf:512 * (nf + 1)],
                                          psc[a % 2][:]).then_inc(seva, 1)


def _pw_quant(nc, g, w_in, wq_sp, rows, cols, swr, swr_col, ident_f, tap, tname):
    ntile = rows // 128
    with ExitStack() as es:
        p = P(nc, es)
        wt = [p.sb(f"w_{tname}_t{i}", [128, cols], F32) for i in range(2)]
        wi = [p.sb(f"w_{tname}_i{i}", [128, cols], I32) for i in range(2)]
        wb = [p.sb(f"w_{tname}_b{i}", [128, cols], BF16) for i in range(2)]
        acc = p.sb(f"w_{tname}_acc", [128, ntile], F32)
        wf = p.sb(f"w_{tname}_wf", [128, cols], F32)
        st = p.sb(f"w_{tname}_st", [128, 2], F32)
        onecol = p.sb(f"w_{tname}_oc", [128, 1], F32)
        onerow = p.sb(f"w_{tname}_or", [1, 128], F32)
        pp = p.ps(f"w_{tname}_pp", [128, 8], F32)
        sx = [p.sem(f"w_{tname}_sx0"), p.sem(f"w_{tname}_sx1")]
        sv = p.sem(f"w_{tname}_sv"); smm = p.sem(f"w_{tname}_smm")
        sg = p.sem(f"w_{tname}_sg", sw=True); sq = [p.sem(f"w_{tname}_sq0"), p.sem(f"w_{tname}_sq1")]
        sgo = p.sem(f"w_{tname}_sgo", sw=True)
        p.clear_sems()

        with nc.Block() as blk:
            @blk.sync
            def _(e):
                for t in range(ntile):
                    if t >= 2:
                        e.wait_ge(sv, t - 1)
                    e.dma_start(out=wt[t % 2][:], in_=w_in[128 * t:128 * (t + 1), :]).then_inc(sx[t % 2], 16)
                for t in range(ntile):
                    if t >= 2:
                        e.wait_ge(sv, ntile + 2 + t)
                        e.wait_ge(sg, t - 1)
                        e.dma_start(out=wq_sp[128 * (t - 2):128 * (t - 1), :],
                                    in_=wb[t % 2][:]).then_inc(sq[t % 2], 16)
                    e.dma_start(out=wt[t % 2][:], in_=w_in[128 * t:128 * (t + 1), :]).then_inc(sx[t % 2], 16)
                for t in range(ntile - 2, ntile):
                    e.wait_ge(sg, t + 1)
                    e.dma_start(out=wq_sp[128 * t:128 * (t + 1), :], in_=wb[t % 2][:]).then_inc(sq[t % 2], 16)
                e.wait_ge(sq[0], 16 * ((ntile + 1) // 2))
                e.wait_ge(sq[1], 16 * (ntile // 2))

            @blk.gpsimd
            def _(e):
                e.memset(onecol[:], 1.0)
                e.memset(onerow[:], 1.0).then_inc(sgo, 1)
                for t in range(ntile):
                    e.wait_ge(sv, ntile + 4 + t)
                    if t >= 2:
                        e.wait_ge(sq[t % 2], 16 * ((t - 2) // 2 + 1))
                    e.tensor_scalar(out=wb[t % 2][:], in0=wi[t % 2][:],
                                    scalar1=1, scalar2=-1,
                                    op0=ALU.min, op1=ALU.max).then_inc(sg, 1)

            @blk.tensor
            def _(e):
                e.wait_ge(sgo, 1)
                e.wait_ge(sv, ntile + 1)
                e.matmul(pp[0:1, 0:1], st[:, 1:2], onecol[:], start=True, stop=True).then_inc(smm, 1)
                e.wait_ge(sv, ntile + 2)
                e.matmul(pp[0:128, 4:5], onerow[:], st[0:1, 0:1], start=True, stop=True).then_inc(smm, 1)

            @blk.vector
            def _(e):
                for t in range(ntile):
                    e.wait_ge(sx[t % 2], 16 * (t // 2 + 1))
                    e.tensor_reduce(acc[:, t:t + 1], wt[t % 2][:], axis=AX, op=ALU.add,
                                    apply_absolute_value=True).then_inc(sv, 1)
                e.drain()
                e.reduce_sum(st[:, 1:2], acc[:, 0:ntile], axis=AX).then_inc(sv, 1)
                e.wait_ge(smm, 1)
                e.tensor_single_scalar(out=st[0:1, 1:2], in_=pp[0:1, 0:1],
                                       scalar=1.0 / (rows * cols), op=ALU.mult)
                e.drain()
                e.tensor_single_scalar(out=st[0:1, 1:2], in_=st[0:1, 1:2], scalar=1e-5, op=ALU.max)
                e.drain()
                e.reciprocal(st[0:1, 0:1], st[0:1, 1:2]).then_inc(sv, 1)
                e.wait_ge(smm, 2)
                e.tensor_copy(st[:, 0:1], pp[:, 4:5])
                e.drain()
                e.reciprocal(swr[:, swr_col:swr_col + 1], st[:, 0:1]).then_inc(sv, 1)
                e.drain()
                for t in range(ntile):
                    e.wait_ge(sx[t % 2], 16 * (ntile // 2 + t // 2 + 1))
                    if t >= 2:
                        e.wait_ge(sg, t - 1)
                    e.tensor_scalar(out=wf[:], in0=wt[t % 2][:],
                                    scalar1=st[:, 0:1], scalar2=MAGIC,
                                    op0=ALU.mult, op1=ALU.add)
                    e.drain()
                    e.tensor_single_scalar(out=wi[t % 2][:], in_=wf[:],
                                           scalar=-MAGIC, op=ALU.add).then_inc(sv, 1)


def _pw_bf16(nc, g, w_in, wb_sp, rows, cols, tag):
    ntile = rows // 128
    with ExitStack() as es:
        p = P(nc, es)
        wf = [p.sb(f"wb{tag}_f{i}", [128, cols], F32) for i in range(2)]
        wb = [p.sb(f"wb{tag}_b{i}", [128, cols], BF16) for i in range(2)]
        sg = [p.sem(f"wb{tag}_g{i}", sw=True) for i in range(2)]
        sc = p.sem(f"wb{tag}_c")
        sq = [p.sem(f"wb{tag}_q{i}") for i in range(2)]
        p.clear_sems()
        with nc.Block() as blk:
            @blk.gpsimd
            def _(e):
                for t in range(ntile):
                    if t >= 2:
                        e.wait_ge(sc, t - 1)
                    e.dma_start(out=wf[t % 2][:], in_=w_in[128 * t:128 * (t + 1), :]).then_inc(sg[t % 2], 16)
            @blk.vector
            def _(e):
                for t in range(ntile):
                    e.wait_ge(sg[t % 2], 16 * (t // 2 + 1))
                    if t >= 2:
                        e.wait_ge(sq[t % 2], 16 * ((t - 2) // 2 + 1))
                    e.tensor_copy(wb[t % 2][:], wf[t % 2][:]).then_inc(sc, 1)
            @blk.sync
            def _(e):
                for t in range(ntile):
                    e.wait_ge(sc, t + 1)
                    e.dma_start(out=wb_sp[128 * t:128 * (t + 1), :], in_=wb[t % 2][:]).then_inc(sq[t % 2], 16)
                e.wait_ge(sq[0], 16 * ((ntile + 1) // 2))
                e.wait_ge(sq[1], 16 * (ntile // 2))


def _p4(nc, g, xqT_sp, attn_sp, x_in, x1_sp, wgq_sp, wuq_sp, wdq_sp,
        invsx, invsa, swr, stat_m1, stat_r1, ident_bf, tap):
    tp_x1 = tap("x1", [T, D], F32)
    with ExitStack() as es:
        p = P(nc, es)
        wgT = p.sb("p4_wg", [128, 8 * INNER], FP8)
        wuT = p.sb("p4_wu", [128, 8 * INNER], FP8)
        wdT = p.sb("p4_wd", [128, 32 * D], FP8)
        xqt = [p.sb(f"p4_xq{i}", [128, 8 * 128], BF16) for i in range(2)]
        att = [p.sb(f"p4_at{i}", [128, D], F32) for i in range(2)]
        xt = [p.sb(f"p4_xt{i}", [128, D], F32) for i in range(2)]
        gate_f = p.sb("p4_gate", [128, INNER], F32)
        gate_f2 = p.sb("p4_gate2", [128, INNER], F32)
        gf = [gate_f, gate_f2]
        up_f = p.sb("p4_up", [128, INNER], F32)
        aq_bf = p.sb("p4_aqb", [128, INNER], BF16)
        aqT = p.sb("p4_aqT", [128, INNER], BF16)
        x1st = [p.sb(f"p4_x1{i}", [128, D], F32) for i in range(2)]
        junk = p.sb("p4_junk", [128, D], F32)
        stc = p.sb("p4_stc", [128, 8], F32)
        scol = p.sb("p4_scol", [128, NT], F32)
        qcol = p.sb("p4_qcol", [128, NT], F32)
        tcol = p.sb("p4_tcol", [128, NT], F32)
        pbA = [p.ps(f"p4_pb{i}", [128, 512], F32) for i in range(4)]
        ptr = [p.ps(f"p4_ptr{i}", [128, 512], BF16) for i in range(2)]
        sxq = [p.sem("p4_sxq0"), p.sem("p4_sxq1")]
        sat = [p.sem("p4_sat0"), p.sem("p4_sat1")]
        sxx = [p.sem("p4_sxx0"), p.sem("p4_sxx1")]
        swt = p.sem("p4_swt"); swc = p.sem("p4_swc")
        svc = p.sem("p4_svc")
        sgm = p.sem("p4_sgm"); sum_ = p.sem("p4_sum"); sdm = p.sem("p4_sdm")
        sevS = p.sem("p4_sevS"); sevU = p.sem("p4_sevU")
        sevQ = p.sem("p4_sevQ"); sevT = p.sem("p4_sevT")
        sgp = p.sem("p4_sgp", sw=True); strp = p.sem("p4_strp")
        sevX = p.sem("p4_sevX")
        svst = p.sem("p4_svst")
        ssq = p.sem("p4_ssq"); ssn = p.sem("p4_ssn")
        sp1 = [p.sem("p4_sp10"), p.sem("p4_sp11")]

        wstg_all = gate_f[:].bitcast(BF16)
        wstg = [wstg_all[:, 0:4096], wstg_all[:, 4096:8192]]
        if True:
            p.clear_sems()
        with nc.Block() as blk:
                @blk.sync
                def _(e):
                    k = 0
                    for sp, nch, cw in [(wgq_sp, 8, INNER), (wuq_sp, 8, INNER), (wdq_sp, 32, D)]:
                        for c in range(nch):
                            if k >= 2:
                                e.wait_ge(swc, k - 1)
                            e.dma_start(out=wstg[k % 2][:, 0:cw],
                                        in_=sp[:, 128 * c:128 * (c + 1)],
                                        transpose=True).then_inc(swt, 16)
                            k += 1

                @blk.vector
                def _(e):
                    for k in range(48):
                        e.wait_ge(swt, 16 * (k + 1))
                        if k < 8:
                            dst, cw = wgT[:, INNER * k:INNER * (k + 1)], INNER
                        elif k < 16:
                            dst, cw = wuT[:, INNER * (k - 8):INNER * (k - 7)], INNER
                        else:
                            dst, cw = wdT[:, D * (k - 16):D * (k - 15)], D
                        e.tensor_copy(dst, wstg[k % 2][:, 0:cw]).then_inc(swc, 1)

        stc2 = p.sb("p4_stc2", [128, 8], F32)
        pbD = [p.ps(f"p4_pd{i}", [128, 512], F32) for i in range(2)]
        with nc.Block() as blk:
            @blk.sync
            def _(e):
                for t in range(NT):
                    if t >= 2:
                        e.wait_ge(sum_, 8 * (t - 1))
                        e.wait_ge(sevX, 2 * (t - 1))
                    e.dma_start(out=xqt[t % 2][:], in_=xqT_sp[t, :, :]).then_inc(sxq[t % 2], 16)
                    e.dma_start(out=att[t % 2][:], in_=attn_sp[128 * t:128 * (t + 1), :]).then_inc(sat[t % 2], 16)
                    e.dma_start(out=xt[t % 2][:], in_=x_in[128 * t:128 * (t + 1), :]).then_inc(sxx[t % 2], 16)
                    if t >= 1:
                        e.wait_ge(sevX, 2 * t)
                        e.wait_ge(ssq, t)
                        e.wait_ge(svst, t)
                        e.dma_start(out=x1_sp[128 * (t - 1):128 * t, :],
                                    in_=x1st[(t - 1) % 2][:]).then_inc(sp1[(t - 1) % 2], 16)
                e.wait_ge(sevX, 2 * NT)
                e.wait_ge(ssq, NT)
                e.wait_ge(svst, NT)
                e.dma_start(out=x1_sp[128 * (NT - 1):128 * NT, :],
                            in_=x1st[(NT - 1) % 2][:]).then_inc(sp1[(NT - 1) % 2], 16)
                e.wait_ge(sp1[0], 16 * (NT // 2)); e.wait_ge(sp1[1], 16 * (NT // 2))
                if tp_x1 is not None:
                    e.dma_start(out=tp_x1[:, :], in_=x1_sp[:, :]).then_inc(sp1[0], 16)
                    e.wait_ge(sp1[0], 16 * (NT // 2) + 16)

            @blk.tensor
            def _(e):
                e.wait_ge(swc, 48)

                def transposes(s):
                    e.wait_ge(sgp, s + 1)
                    for grp in range(8):
                        tg = 8 * s + grp
                        if tg >= 2:
                            e.wait_ge(sevT, tg - 1)
                        for j in range(4):
                            c = 4 * grp + j
                            ins = e.transpose(ptr[grp % 2][:, 128 * j:128 * (j + 1)],
                                              aq_bf[:, 128 * c:128 * (c + 1)], ident_bf[:])
                            if j == 3:
                                ins.then_inc(strp, 1)

                def down(s):
                    e.wait_ge(sevT, 8 * (s + 1))
                    for nf in range(2):
                        du = 2 * s + nf
                        if du >= 2:
                            e.wait_ge(sevX, du - 1)
                        for k in range(32):
                            e.matmul(pbD[nf][:], aqT[:, 128 * k:128 * (k + 1)],
                                     wdT[:, D * k + 512 * nf:D * k + 512 * (nf + 1)],
                                     start=(k == 0), stop=(k == 31)).then_inc(sdm, 1 if k == 31 else 0)

                for t in range(NT):
                    e.wait_ge(sxq[t % 2], 16 * (t // 2 + 1))
                    for nb in range(8):
                        gu = 8 * t + nb
                        if gu >= 2:
                            e.wait_ge(sevS, gu - 1)
                        for k in range(8):
                            e.matmul(pbA[nb % 2][:], xqt[t % 2][:, 128 * k:128 * (k + 1)],
                                     wgT[:, INNER * k + 512 * nb:INNER * k + 512 * (nb + 1)],
                                     start=(k == 0), stop=(k == 7)).then_inc(sgm, 1 if k == 7 else 0)
                    if t >= 1:
                        transposes(t - 1)
                    for nb in range(8):
                        gu = 8 * t + nb
                        if gu >= 2:
                            e.wait_ge(sevU, gu - 1)
                        for k in range(8):
                            e.matmul(pbA[2 + nb % 2][:], xqt[t % 2][:, 128 * k:128 * (k + 1)],
                                     wuT[:, INNER * k + 512 * nb:INNER * k + 512 * (nb + 1)],
                                     start=(k == 0), stop=(k == 7)).then_inc(sum_, 1 if k == 7 else 0)
                    if t >= 1:
                        down(t - 1)
                transposes(NT - 1)
                down(NT - 1)

            @blk.scalar
            def _(e):
                for t in range(NT):
                    e.wait_ge(svc, t + 1)
                    for nb in range(8):
                        e.wait_ge(sgm, 8 * t + nb + 1)
                        if t >= 2:
                            e.wait_ge(sevQ, t - 1)
                        e.activation(out=gf[t % 2][:, 512 * nb:512 * (nb + 1)], in_=pbA[nb % 2][:],
                                     func=AF.Silu, scale=stc[:, 0:1]).then_inc(sevS, 1)
                    if t >= 1:
                        e.wait_ge(sevX, 2 * t)
                        e.activation(out=junk[:], in_=x1st[(t - 1) % 2][:], func=AF.Square).then_inc(ssq, 1)
                e.wait_ge(sevX, 2 * NT)
                e.activation(out=junk[:], in_=x1st[(NT - 1) % 2][:], func=AF.Square).then_inc(ssq, 1)
                e.wait_ge(svst, NT + 1)
                e.activation(out=stat_r1[:], in_=qcol[:], func=AF.Sqrt).then_inc(ssn, 1)

            @blk.gpsimd
            def _(e):
                for t in range(NT):
                    e.wait_ge(sevQ, t + 1)
                    if t >= 1:
                        e.wait_ge(strp, 8 * t)
                    e.tensor_scalar(out=aq_bf[:], in0=up_f[:], scalar1=127, scalar2=-128,
                                    op0=ALU.min, op1=ALU.max).then_inc(sgp, 1)

            @blk.vector
            def _(e):
                def x1evac(s):
                    e.wait_ge(sat[s % 2], 16 * (s // 2 + 1))
                    e.wait_ge(sxx[s % 2], 16 * (s // 2 + 1))
                    if s >= 2:
                        e.wait_ge(sp1[s % 2], 16 * ((s - 2) // 2 + 1))
                    for nf in range(2):
                        e.wait_ge(sdm, 2 * s + nf + 1)
                        e.scalar_tensor_tensor(out=x1st[s % 2][:, 512 * nf:512 * (nf + 1)],
                                               in0=pbD[nf][:], scalar=stc[:, 5 + s % 2:6 + s % 2],
                                               in1=att[s % 2][:, 512 * nf:512 * (nf + 1)],
                                               op0=ALU.mult, op1=ALU.add)
                        e.drain()
                        e.tensor_tensor(out=x1st[s % 2][:, 512 * nf:512 * (nf + 1)],
                                        in0=x1st[s % 2][:, 512 * nf:512 * (nf + 1)],
                                        in1=xt[s % 2][:, 512 * nf:512 * (nf + 1)],
                                        op=ALU.add).then_inc(sevX, 1)
                    e.drain()
                    e.reduce_sum(scol[:, s:s + 1], x1st[s % 2][:], axis=AX)
                    e.wait_ge(ssq, s + 1)
                    e.reduce_sum(qcol[:, s:s + 1], junk[:], axis=AX).then_inc(svst, 1)

                for t in range(NT):
                    e.tensor_tensor(out=stc[:, 0:1], in0=invsx[:, t:t + 1], in1=swr[:, 0:1], op=ALU.mult)
                    e.tensor_tensor(out=stc[:, 1:2], in0=invsx[:, t:t + 1], in1=swr[:, 1:2], op=ALU.mult)
                    e.drain()
                    e.nop().then_inc(svc, 1)
                    if t >= 1:
                        if t >= 2:
                            e.wait_ge(sdm, 2 * (t - 1))
                        for grp in range(8):
                            e.wait_ge(strp, 8 * (t - 1) + grp + 1)
                            e.tensor_copy(aqT[:, 512 * grp:512 * (grp + 1)], ptr[grp % 2][:]).then_inc(sevT, 1)
                    for nb in range(8):
                        e.wait_ge(sevS, 8 * t + nb + 1)
                        e.wait_ge(sum_, 8 * t + nb + 1)
                        e.scalar_tensor_tensor(out=gf[t % 2][:, 512 * nb:512 * (nb + 1)],
                                               in0=pbA[2 + nb % 2][:], scalar=stc[:, 1:2],
                                               in1=gf[t % 2][:, 512 * nb:512 * (nb + 1)],
                                               op0=ALU.mult, op1=ALU.mult).then_inc(sevU, 1)
                        e.drain()
                        e.reduce_max(stc2[:, nb:nb + 1], gf[t % 2][:, 512 * nb:512 * (nb + 1)],
                                     axis=AX, apply_absolute_value=True)
                    e.drain()
                    e.reduce_max(stc[:, 3:4], stc2[:], axis=AX, apply_absolute_value=True)
                    e.drain()
                    e.tensor_single_scalar(out=stc[:, 3:4], in_=stc[:, 3:4], scalar=1e-5, op=ALU.max)
                    e.drain()
                    e.reciprocal(stc[:, 4:5], stc[:, 3:4])
                    e.drain()
                    e.tensor_single_scalar(out=stc[:, 4:5], in_=stc[:, 4:5], scalar=127.0, op=ALU.mult)
                    e.tensor_single_scalar(out=invsa[:, t:t + 1], in_=stc[:, 3:4], scalar=1.0 / 127.0, op=ALU.mult)
                    e.drain()
                    e.tensor_tensor(out=stc[:, 5 + t % 2:6 + t % 2], in0=invsa[:, t:t + 1],
                                    in1=swr[:, 2:3], op=ALU.mult)
                    if t >= 1:
                        e.wait_ge(sgp, t)
                    e.tensor_scalar(out=up_f[:], in0=gf[t % 2][:],
                                    scalar1=stc[:, 4:5], scalar2=MAGIC,
                                    op0=ALU.mult, op1=ALU.add)
                    e.drain()
                    e.tensor_single_scalar(out=up_f[:], in_=up_f[:],
                                           scalar=-MAGIC, op=ALU.add).then_inc(sevQ, 1)
                    if t >= 1:
                        x1evac(t - 1)
                e.wait_ge(sdm, 2 * (NT - 1))
                for grp in range(8):
                    e.wait_ge(strp, 8 * (NT - 1) + grp + 1)
                    e.tensor_copy(aqT[:, 512 * grp:512 * (grp + 1)], ptr[grp % 2][:]).then_inc(sevT, 1)
                x1evac(NT - 1)
                e.drain()
                e.tensor_single_scalar(out=scol[:], in_=scol[:], scalar=1.0 / D, op=ALU.mult)
                e.tensor_single_scalar(out=qcol[:], in_=qcol[:], scalar=1.0 / D, op=ALU.mult)
                e.drain()
                e.tensor_tensor(out=tcol[:], in0=scol[:], in1=scol[:], op=ALU.mult)
                e.drain()
                e.tensor_sub(tcol[:], qcol[:], tcol[:])
                e.drain()
                e.tensor_single_scalar(out=tcol[:], in_=tcol[:], scalar=1e-5, op=ALU.add)
                e.drain()
                e.reciprocal(qcol[:], tcol[:]).then_inc(svst, 1)
                e.wait_ge(ssn, 1)
                for _ in range(2):
                    e.tensor_tensor(out=qcol[:], in0=stat_r1[:], in1=stat_r1[:], op=ALU.mult)
                    e.drain()
                    e.tensor_tensor(out=qcol[:], in0=qcol[:], in1=tcol[:], op=ALU.mult)
                    e.drain()
                    e.tensor_scalar(out=qcol[:], in0=qcol[:], scalar1=-0.5, scalar2=1.5,
                                    op0=ALU.mult, op1=ALU.add)
                    e.drain()
                    e.tensor_tensor(out=stat_r1[:], in0=stat_r1[:], in1=qcol[:], op=ALU.mult)
                    e.drain()
                e.tensor_tensor(out=stat_m1[:], in0=scol[:], in1=stat_r1[:], op=ALU.mult)
                e.drain()
                e.tensor_single_scalar(out=stat_m1[:], in_=stat_m1[:], scalar=-1.0, op=ALU.mult)


def _p5_evac(e, t, ptr, xn1T, xn1T_own, bc, strp, sevT):
    j = t % 16
    for grp in range(2):
        e.wait_ge(strp, 2 * t + grp + 1)
        for jj in range(4):
            c = 4 * grp + jj
            e.tensor_copy(xn1T[:, T * c + 128 * t:T * c + 128 * (t + 1)],
                          ptr[grp][:, 128 * jj:128 * (jj + 1)])
        for jj in range(4):
            c = 4 * grp + jj
            dst = xn1T_own[:, TOWN * c + 128 * j:TOWN * c + 128 * (j + 1)]
            srcp = ptr[grp][:, 128 * jj:128 * (jj + 1)]
            if t < 16:
                ins = e.tensor_single_scalar(out=dst, in_=srcp, scalar=bc[:, 4:5], op=ALU.mult)
            else:
                ins = e.scalar_tensor_tensor(out=dst, in0=srcp, scalar=bc[:, 3:4],
                                             in1=dst, op0=ALU.mult, op1=ALU.add)
            if jj == 3:
                ins.then_inc(sevT, 1)


def _p5(nc, g, x1_sp, stat_m1, stat_r1, xn1T_own, rcatq_own, tpTb,
        lt_sb, M_sb, E2b, oddT, bc,
        wQ1_in, wK1_in, winv_in, vb_in, wb_in, vt_in, wt_in, xt_in, rcf_in,
        scal_sb, ident_bf, ident_f, onerow_f, tap):
    with ExitStack() as es:
        p = P(nc, es)
        acat = p.sb("p5_acat", [128, 8 * 512], BF16)
        projk = p.sb("p5_projk", [128, 2 * T], F32)
        projq = p.sb("p5_projq", [128, 2 * T], BF16)
        tpT_own = p.sb("p5_tpT", [4, TOWN], F32)
        pb = [p.ps(f"p5_pb{i}", [128, 512], F32) for i in range(4)]
        ptr = [p.ps(f"p5_ptr{i}", [128, 512], BF16) for i in range(2)]
        swl = p.sem("p5_swl", sw=True); sgw = p.sem("p5_sgw", sw=True)
        smmA = p.sem("p5_smmA"); sevA = p.sem("p5_sevA")
        sx1 = [p.sem("p5_sx10"), p.sem("p5_sx11")]
        sv = p.sem("p5_sv"); strp = p.sem("p5_strp"); sevT = p.sem("p5_sevT")
        slt = p.sem("p5_slt"); sltg = p.sem("p5_sltg", sw=True); sms = p.sem("p5_sms", sw=True)
        smmP = p.sem("p5_smmP"); sevP = p.sem("p5_sevP")
        p.clear_sems()

        with ExitStack() as esw:
            pw = P(nc, esw)
            wk1 = pw.sb("p5_wk1", [128, 8 * D], BF16)
            wq1 = pw.sb("p5_wq1", [128, 8 * D], BF16)
            wnv = pw.sb("p5_wnv", [128, 8 * D], BF16)
            smal = pw.sb("p5_smal", [128, 8 * 64 * 5], BF16)
            bwv = pw.sb("p5_bwv", [128, 8 * 128], BF16)
            with nc.Block() as blk:
                @blk.gpsimd
                def _(e):
                    for wbf, win in ((wk1, wK1_in), (wq1, wQ1_in), (wnv, winv_in)):
                        for k in range(8):
                            e.dma_start(out=wbf[:, D * k:D * (k + 1)],
                                        in_=win[128 * k:128 * (k + 1), :]).then_inc(swl, 16)
                    for j, win in enumerate((vb_in, wb_in, vt_in, wt_in, xt_in)):
                        for k in range(8):
                            e.dma_start(out=smal[:, 64 * (8 * j + k):64 * (8 * j + k + 1)],
                                        in_=win[128 * k:128 * (k + 1), :]).then_inc(swl, 16)
                    for m in range(8):
                        ins = e.memset(acat[:, 512 * m + 448:512 * (m + 1)], 0.0)
                    ins.then_inc(sgw, 1)

                @blk.tensor
                def _(e):
                    e.wait_ge(swl, 16 * 64)
                    e.wait_ge(sgw, 1)
                    u = 0
                    for m in range(8):
                        for si, srcj in enumerate((1, 0)):
                            if u >= 4:
                                e.wait_ge(sevA, u - 3)
                            for k in range(8):
                                e.matmul(pb[u % 4][:, 0:64],
                                         wnv[:, D * k + 128 * m:D * k + 128 * (m + 1)],
                                         smal[:, 64 * (8 * srcj + k):64 * (8 * srcj + k + 1)],
                                         start=(k == 0), stop=(k == 7)).then_inc(smmA, 1 if k == 7 else 0)
                            u += 1
                    e.wait_ge(sevA, 16)
                    for m in range(8):
                        for gi, (wbf, srcs) in enumerate((
                                (wk1, [("s", 1), ("b", 1), ("s", 3)]),
                                (wq1, [("s", 0), ("b", 0), ("s", 2)]))):
                            for si, (kind, j) in enumerate(srcs):
                                if u >= 4:
                                    e.wait_ge(sevA, u - 3)
                                for k in range(8):
                                    if kind == "s":
                                        rhs = smal[:, 64 * (8 * j + k):64 * (8 * j + k + 1)]
                                    else:
                                        rhs = bwv[:, 128 * k + 64 * j:128 * k + 64 * (j + 1)]
                                    e.matmul(pb[u % 4][:, 0:64],
                                             wbf[:, D * k + 128 * m:D * k + 128 * (m + 1)],
                                             rhs, start=(k == 0), stop=(k == 7)).then_inc(smmA, 1 if k == 7 else 0)
                                u += 1
                    e.wait_ge(sevA, u)
                    for j in range(5):
                        e.matmul(pb[0][:, j:j + 1], onerow_f[:], scal_sb[0:1, j:j + 1],
                                 start=True, stop=True).then_inc(smmA, 1)

                @blk.vector
                def _(e):
                    u = 0
                    for m in range(8):
                        for si in range(2):
                            e.wait_ge(smmA, u + 1)
                            e.tensor_copy(bwv[:, 128 * m + 64 * si:128 * m + 64 * (si + 1)],
                                          pb[u % 4][:, 0:64]).then_inc(sevA, 1)
                            u += 1
                    for m in range(8):
                        for gi in range(2):
                            for si in range(3):
                                e.wait_ge(smmA, u + 1)
                                col = 256 * gi + 64 * si if gi == 1 else 64 * si
                                e.tensor_copy(acat[:, 512 * m + col:512 * m + col + 64],
                                              pb[u % 4][:, 0:64]).then_inc(sevA, 1)
                                u += 1
                    for m in range(8):
                        e.tensor_copy(acat[:, 512 * m + 192:512 * m + 256],
                                      smal[:, 64 * (8 * 4 + m):64 * (8 * 4 + m + 1)])
                    e.wait_ge(smmA, 16 + 48 + 5)
                    e.tensor_copy(bc[:, 0:5], pb[0][:, 0:5])
                    e.drain()

        with ExitStack() as esx:
            px = P(nc, esx)
            xn1T = px.sb("p5_xn1T", [128, 8 * T], BF16)
            x1t = [px.sb(f"p5_x1t{i}", [128, D], F32) for i in range(2)]
            xn_f = px.sb("p5_xnf", [128, D], F32)
            xn_bf = [px.sb(f"p5_xnbf{i}", [128, D], BF16) for i in range(2)]
            with nc.Block() as blk:
                @blk.sync
                def _(e):
                    for t in range(NT):
                        if t >= 2:
                            e.wait_ge(sv, t - 1)
                        e.dma_start(out=x1t[t % 2][:], in_=x1_sp[128 * t:128 * (t + 1), :]).then_inc(sx1[t % 2], 16)
                        if t == 1:
                            e.wait_ge(sv, 1)
                            e.dma_start(out=lt_sb[:, :], in_=xn_bf[0][1:17, :]).then_inc(slt, 16)

                @blk.gpsimd
                def _(e):
                    e.memset(M_sb[:], 0.0)
                    e.memset(E2b[:], 0.0).then_inc(sms, 1)
                    e.wait_ge(sms, 1)
                    e.wait_ge(slt, 16)
                    for q in range(4):
                        for ci in range(4):
                            e.dma_start(out=E2b[4 * q + ci:4 * q + ci + 1, 256 * q:256 * (q + 1)],
                                        in_=lt_sb[8 + 2 * q:9 + 2 * q, ci::4],
                                        allow_slow_non_contiguous=True).then_inc(sltg, 16)
                    e.wait_ge(sltg, 16 * 16)

                @blk.vector
                def _(e):
                    for t in range(NT):
                        e.wait_ge(sx1[t % 2], 16 * (t // 2 + 1))
                        e.tensor_scalar(out=xn_f[:], in0=x1t[t % 2][:],
                                        scalar1=stat_r1[:, t:t + 1], scalar2=stat_m1[:, t:t + 1],
                                        op0=ALU.mult, op1=ALU.add)
                        e.drain()
                        if t >= 2:
                            e.wait_ge(strp, 2 * (t - 2) + 2)
                        if t == 2:
                            e.wait_ge(slt, 16)
                        e.tensor_copy(xn_bf[t % 2][:], xn_f[:]).then_inc(sv, 1)
                        if t >= 1:
                            _p5_evac(e, t - 1, ptr, xn1T, xn1T_own, bc, strp, sevT)
                    _p5_evac(e, NT - 1, ptr, xn1T, xn1T_own, bc, strp, sevT)
                    for c in range(8):
                        e.wait_ge(strp, 2 * NT + c + 1)
                        sl = ptr[0][:, 16 * c:16 * (c + 1)].rearrange("p (r two) -> p two r", two=2)
                        e.tensor_copy(oddT[:, 4 * c:4 * (c + 1)], sl[:, 1, 0:4]).then_inc(sevT, 1)
                        e.tensor_copy(oddT[:, 32 + 4 * c:32 + 4 * (c + 1)], sl[:, 1, 4:8]).then_inc(sevT, 1)
                    e.drain()
                    e.wait_ge(sms, 1)
                    for mi in range(32):
                        dst = M_sb[:].rearrange("p (c q f) -> p c f q", c=32, q=4)[:, mi, mi // 8, :]
                        e.tensor_copy(dst, oddT[:, 32 + 4 * (mi % 8):32 + 4 * (mi % 8) + 4])
                    e.drain()
                    u = 0
                    for m in range(4):
                        for n in range(8):
                            e.wait_ge(smmP, u + 1)
                            if m < 2:
                                dst = projk[:, T * m + 512 * n:T * m + 512 * (n + 1)]
                            else:
                                dst = projq[:, T * (m - 2) + 512 * n:T * (m - 2) + 512 * (n + 1)]
                            e.tensor_copy(dst, pb[u % 4][:]).then_inc(sevP, 1)
                            u += 1
                    for n in range(4):
                        e.wait_ge(smmP, u + 1)
                        e.tensor_copy(tpT_own[0:4, 512 * n:512 * (n + 1)], pb[u % 4][0:4, :]).then_inc(sevP, 1)
                        u += 1

                @blk.tensor
                def _(e):
                    for t in range(NT):
                        e.wait_ge(sv, t + 1)
                        if t >= 1:
                            e.wait_ge(sevT, 2 * (t - 1) + 2)
                        for grp in range(2):
                            for j in range(4):
                                c = 4 * grp + j
                                ins = e.transpose(ptr[grp][:, 128 * j:128 * (j + 1)],
                                                  xn_bf[t % 2][:, 128 * c:128 * (c + 1)], ident_bf[:])
                                if j == 3:
                                    ins.then_inc(strp, 1)
                    e.wait_ge(sevT, 2 * NT)
                    e.wait_ge(slt, 16)
                    for c in range(8):
                        e.transpose(ptr[0][:, 16 * c:16 * (c + 1)],
                                    lt_sb[:, 128 * c:128 * (c + 1)],
                                    ident_bf[0:16, 0:16]).then_inc(strp, 1)
                    u = 0
                    for m in range(4):
                        for n in range(8):
                            if u >= 4:
                                e.wait_ge(sevP, u - 3)
                            for k in range(8):
                                e.matmul(pb[u % 4][:],
                                         acat[:, 512 * k + 128 * m:512 * k + 128 * (m + 1)],
                                         xn1T[:, T * k + 512 * n:T * k + 512 * (n + 1)],
                                         start=(k == 0), stop=(k == 7)).then_inc(smmP, 1 if k == 7 else 0)
                            u += 1
                    e.wait_ge(sevT, 2 * NT + 16)
                    for n in range(4):
                        if u >= 4:
                            e.wait_ge(sevP, u - 3)
                        for k in range(8):
                            e.matmul(pb[u % 4][0:4, :], oddT[:, 4 * k:4 * (k + 1)],
                                     xn1T_own[:, TOWN * k + 512 * n:TOWN * k + 512 * (n + 1)],
                                     start=(k == 0), stop=(k == 7)).then_inc(smmP, 1 if k == 7 else 0)
                        u += 1

        with ExitStack() as esd:
            pd = P(nc, esd)
            cum = pd.sb("p5_cum", [128, 2 * T], F32)
            rep64 = pd.sb("p5_rep", [64, T], F32)
            rcat = pd.sb("p5_rcat", [128, T], F32)
            rcfs = pd.sb("p5_rcf", [1, T], F32)
            rcq = pd.sb("p5_rcq", [128, TOWN], F32)
            s5c = pd.sem("p5_s5c")
            sdc4 = pd.sem("p5_sdc4"); sdinv = pd.sem("p5_sdinv"); sdtri = pd.sem("p5_sdtri")
            srcf = pd.sem("p5_srcf")
            pd.clear_sems()
            with nc.Block() as blk:
                @blk.sync
                def _(e):
                    e.dma_start(out=rcfs[:], in_=rcf_in[:, :]).then_inc(srcf, 16)
                    e.wait_ge(s5c, 1)
                    e.dma_start(out=projk[0:64, 0:T], in_=cum[64:128, T:2 * T]).then_inc(sdc4, 16)
                    e.wait_ge(s5c, 2)
                    e.dma_start(out=cum[0:64, 0:T], in_=rcat[64:128, :]).then_inc(sdinv, 16)
                    e.wait_ge(s5c, 4)
                    e.dma_start(out=rcat[64:128, :], in_=projk[0:64, 0:T]).then_inc(sdtri, 16)

                @blk.tensor
                def _(e):
                    e.wait_ge(srcf, 16)
                    for n in range(8):
                        if n >= 4:
                            e.wait_ge(sevP, n + 33)
                        e.matmul(pb[n % 4][0:64, :], onerow_f[0:1, 0:64],
                                 rcfs[0:1, 512 * n:512 * (n + 1)],
                                 start=True, stop=True).then_inc(smmP, 1)

                @blk.vector
                def _(e):
                    for c in range(2):
                        e.tensor_tensor_scan(out=cum[:, T * c:T * (c + 1)],
                                             data0=projk[:, T * c:T * (c + 1)],
                                             data1=projk[:, T * c:T * (c + 1)],
                                             initial=0.0, op0=ALU.add, op1=ALU.bypass)
                    e.drain().then_inc(s5c, 1)
                    for n in range(8):
                        e.wait_ge(smmP, 37 + n)
                        e.tensor_copy(rep64[:, 512 * n:512 * (n + 1)], pb[n % 4][0:64, :]).then_inc(sevP, 1)
                    e.tensor_copy(rcat[:, :], projq[:, 0:T])
                    e.drain()
                    e.tensor_tensor(out=rcat[:, :], in0=rcat[:, :], in1=cum[:, 0:T], op=ALU.mult)
                    e.drain().then_inc(s5c, 1)
                    e.wait_ge(sdinv, 16)
                    e.scalar_tensor_tensor(out=rcat[0:64, :], in0=cum[0:64, 0:T],
                                           scalar=bc[0:64, 0:1], in1=rcat[0:64, :],
                                           op0=ALU.mult, op1=ALU.add)
                    e.drain().then_inc(s5c, 1)
                    e.tensor_copy(projk[0:64, T:2 * T], projq[0:64, T:2 * T])
                    e.drain()
                    e.tensor_tensor(out=projk[0:64, T:2 * T], in0=projk[0:64, T:2 * T],
                                    in1=cum[0:64, T:2 * T], op=ALU.mult)
                    e.wait_ge(sdc4, 16)
                    e.drain()
                    e.tensor_tensor(out=projk[0:64, 0:T], in0=projk[0:64, T:2 * T],
                                    in1=projk[0:64, 0:T], op=ALU.mult)
                    e.drain()
                    e.scalar_tensor_tensor(out=projk[0:64, 0:T], in0=rep64[:, :],
                                           scalar=bc[0:64, 1:2], in1=projk[0:64, 0:T],
                                           op0=ALU.mult, op1=ALU.mult)
                    e.drain().then_inc(s5c, 1)
                    e.wait_ge(sdtri, 16)
                    e.tensor_single_scalar(out=rcq[:], in_=rcat[:, 0:TOWN],
                                           scalar=bc[:, 4:5], op=ALU.mult)
                    e.drain()
                    e.scalar_tensor_tensor(out=rcq[:], in0=rcat[:, TOWN:T],
                                           scalar=bc[:, 3:4], in1=rcq[:],
                                           op0=ALU.mult, op1=ALU.add)
                    e.drain()
                    e.tensor_copy(rcatq_own[:], rcq[:])
                    e.drain()
                    e.tensor_single_scalar(out=E2b[:], in_=E2b[:], scalar=bc[0:16, 2:3], op=ALU.mult)
                    e.tensor_single_scalar(out=tpT_own[0:4, :], in_=tpT_own[0:4, :],
                                           scalar=bc[0:4, 2:3], op=ALU.mult)
                    e.drain()
                    e.tensor_copy(tpTb[0:4, :], tpT_own[0:4, :])
                    e.drain()

    tps = [(tap("rcat", [128, TOWN], BF16), rcatq_own),
           (tap("xn1To", [128, 8 * TOWN], BF16), xn1T_own),
           (tap("E2b", [16, D], BF16), E2b),
           (tap("M", [128, 32 * 16], BF16), M_sb),
           (tap("oddT", [128, 64], BF16), oddT)]
    tps = [(t, s) for t, s in tps if t is not None]
    if tps:
        stap = g.sem("p5_tap")
        with nc.Block() as blk:
            @blk.sync
            def _(e):
                e.sem_clear(stap)
                for t, s in tps:
                    e.dma_start(out=t[:, :], in_=s[:]).then_inc(stap, 16)
                e.wait_ge(stap, 16 * len(tps))


def _p7(nc, g, rcatq_own, x1_sp, x1o_sp, wob_sp, ub_in, ut_in, biasb_in, biast_in,
        bc, rco_sb, onerow_f, ident_bf, tap):
    with ExitStack() as es:
        p = P(nc, es)
        ubs = p.sb("p7_ubs", [128, 8 * 128], BF16)
        ucat = p.sb("p7_ucat", [128, D], BF16)
        wot = p.sb("p7_wot", [128, 8 * D], BF16)
        spre = p.sb("p7_spre", [128, 8 * TOWN], BF16)
        brow = p.sb("p7_brow", [1, 2 * D], F32)
        bcomb = p.sb("p7_bcomb", [1, D], F32)
        bcomb_bf = p.sb("p7_bcombb", [1, D], BF16)
        bcombT = p.sb("p7_bcombT", [128, 8], BF16)
        vwo = p.sb("p7_vwo", [1, D], F32)
        vrep = p.sb("p7_vrep", [128, D], F32)
        x1a = [p.sb(f"p7_x1a{i}", [128, D], F32) for i in range(2)]
        x1b = [p.sb(f"p7_x1b{i}", [128, D], F32) for i in range(2)]
        xcomb = [p.sb(f"p7_xc{i}", [128, D], F32) for i in range(2)]
        xo = [p.sb(f"p7_xo{i}", [128, D], F32) for i in range(2)]
        pb = [p.ps(f"p7_pb{i}", [128, 512], F32) for i in range(4)]
        ptr7 = [p.ps(f"p7_ptr{i}", [128, 512], BF16) for i in range(2)]
        swl = p.sem("p7_swl", sw=True); swt = p.sem("p7_swt", act=True)
        smmU = p.sem("p7_smmU"); sevU = p.sem("p7_sevU"); svb = p.sem("p7_svb")
        sx7 = [p.sem("p7_sx70"), p.sem("p7_sx71")]
        sxc = p.sem("p7_sxc")
        smmO = p.sem("p7_smmO"); sevO = p.sem("p7_sevO")
        sso = [p.sem("p7_sso0"), p.sem("p7_sso1")]
        p.clear_sems()

        with nc.Block() as blk:
            @blk.gpsimd
            def _(e):
                for k in range(8):
                    e.dma_start(out=ubs[:, 128 * k:128 * k + 64],
                                in_=ub_in[128 * k:128 * (k + 1), :]).then_inc(swl, 16)
                    e.dma_start(out=ubs[:, 128 * k + 64:128 * (k + 1)],
                                in_=ut_in[128 * k:128 * (k + 1), :]).then_inc(swl, 16)
                e.dma_start(out=brow[0:1, 0:D], in_=biasb_in[:, :]).then_inc(swl, 16)
                e.dma_start(out=brow[0:1, D:2 * D], in_=biast_in[:, :]).then_inc(swl, 16)

            @blk.scalar
            def _(e):
                for k in range(8):
                    e.dma_start(out=wot[:, D * k:D * (k + 1)],
                                in_=wob_sp[:, 128 * k:128 * (k + 1)],
                                transpose=True).then_inc(swt, 16)

            @blk.sync
            def _(e):
                for t in range(NOWN):
                    if t >= 2:
                        e.wait_ge(sxc, t - 1)
                    e.dma_start(out=x1a[t % 2][:], in_=x1_sp[128 * t:128 * (t + 1), :]).then_inc(sx7[t % 2], 16)
                    e.dma_start(out=x1b[t % 2][:],
                                in_=x1_sp[TOWN + 128 * t:TOWN + 128 * (t + 1), :]).then_inc(sx7[t % 2], 16)
                    if t >= 1:
                        e.wait_ge(sevO, 2 * (t - 1) + 2)
                        e.dma_start(out=x1o_sp[128 * (t - 1):128 * t, :],
                                    in_=xo[(t - 1) % 2][:]).then_inc(sso[(t - 1) % 2], 16)
                e.wait_ge(sevO, 2 * NOWN)
                e.dma_start(out=x1o_sp[128 * (NOWN - 1):128 * NOWN, :],
                            in_=xo[(NOWN - 1) % 2][:]).then_inc(sso[(NOWN - 1) % 2], 16)
                e.wait_ge(sso[0], 16 * (NOWN // 2))
                e.wait_ge(sso[1], 16 * (NOWN // 2))

            @blk.tensor
            def _(e):
                u = 0
                e.wait_ge(swl, 16 * 16)
                for b in range(2):
                    for i in range(4):
                        k = 4 * b + i
                        ins = e.transpose(ptr7[b][:, 128 * i:128 * (i + 1)],
                                          ubs[:, 128 * k:128 * (k + 1)], ident_bf[:])
                    ins.then_inc(smmU, 1)
                    u += 1
                e.wait_ge(svb, 1)
                e.wait_ge(sevU, 2)
                for k in range(8):
                    ins = e.transpose(ptr7[0][:, 2 * k:2 * k + 1],
                                      bcomb_bf[0:1, 128 * k:128 * (k + 1)], ident_bf[0:1, 0:1])
                ins.then_inc(smmU, 1)
                u += 1
                e.wait_ge(swt, 16 * 8)
                e.wait_ge(sevU, u)
                for oh in range(2):
                    if u >= 4:
                        e.wait_ge(sevU, u - 3)
                    for k in range(8):
                        e.matmul(pb[u % 4][0:1, :], bcombT[:, k:k + 1],
                                 wot[:, D * k + 512 * oh:D * k + 512 * (oh + 1)],
                                 start=(k == 0), stop=(k == 7)).then_inc(smmU, 1 if k == 7 else 0)
                    u += 1
                e.wait_ge(sevU, u)
                for oh in range(2):
                    if u >= 4:
                        e.wait_ge(sevU, u - 3)
                    e.matmul(pb[u % 4][:], onerow_f[0:1, 0:128], vwo[0:1, 512 * oh:512 * (oh + 1)],
                             start=True, stop=True).then_inc(smmU, 1)
                    u += 1
                for m in range(8):
                    for n in range(4):
                        if u >= 4:
                            e.wait_ge(sevU, u - 3)
                        e.matmul(pb[u % 4][:], ucat[:, 128 * m:128 * (m + 1)],
                                 rcatq_own[:, 512 * n:512 * (n + 1)],
                                 start=True, stop=True).then_inc(smmU, 1)
                        u += 1
                e.wait_ge(sevU, u)
                uo = 0
                for t in range(NOWN):
                    for oh in range(2):
                        if uo >= 4:
                            e.wait_ge(sevO, uo - 3)
                        for k in range(8):
                            e.matmul(pb[uo % 4][:],
                                     spre[:, TOWN * k + 128 * t:TOWN * k + 128 * (t + 1)],
                                     wot[:, D * k + 512 * oh:D * k + 512 * (oh + 1)],
                                     start=(k == 0), stop=(k == 7)).then_inc(smmO, 1 if k == 7 else 0)
                        uo += 1

            @blk.vector
            def _(e):
                u = 0
                for b in range(2):
                    e.wait_ge(smmU, u + 1)
                    e.tensor_copy(ucat[:, 512 * b:512 * (b + 1)], ptr7[b][:]).then_inc(sevU, 1)
                    u += 1
                e.wait_ge(swl, 16 * 18)
                e.scalar_tensor_tensor(out=bcomb[:], in0=brow[0:1, 0:D], scalar=bc[0:1, 0:1],
                                       in1=brow[0:1, 0:D], op0=ALU.mult, op1=ALU.add)
                e.drain()
                e.scalar_tensor_tensor(out=bcomb[:], in0=brow[0:1, D:2 * D], scalar=bc[0:1, 1:2],
                                       in1=bcomb[:], op0=ALU.mult, op1=ALU.add)
                e.drain()
                e.tensor_copy(bcomb_bf[:], bcomb[:])
                e.drain().then_inc(svb, 1)
                e.wait_ge(smmU, u + 1)
                e.tensor_copy(bcombT[:], ptr7[0][:, 0:16:2]).then_inc(sevU, 1)
                u += 1
                for oh in range(2):
                    e.wait_ge(smmU, u + 1)
                    e.tensor_copy(vwo[0:1, 512 * oh:512 * (oh + 1)], pb[u % 4][0:1, :]).then_inc(sevU, 1)
                    u += 1
                for oh in range(2):
                    e.wait_ge(smmU, u + 1)
                    e.tensor_copy(vrep[:, 512 * oh:512 * (oh + 1)], pb[u % 4][:]).then_inc(sevU, 1)
                    u += 1
                for m in range(8):
                    for n in range(4):
                        e.wait_ge(smmU, u + 1)
                        e.tensor_copy(spre[:, TOWN * m + 512 * n:TOWN * m + 512 * (n + 1)],
                                      pb[u % 4][:]).then_inc(sevU, 1)
                        u += 1
                uo = 0
                for t in range(NOWN):
                    e.wait_ge(sx7[t % 2], 32 * (t // 2 + 1))
                    e.tensor_single_scalar(out=xcomb[t % 2][:], in_=x1a[t % 2][:],
                                           scalar=bc[:, 4:5], op=ALU.mult)
                    e.drain()
                    e.scalar_tensor_tensor(out=xcomb[t % 2][:], in0=x1b[t % 2][:],
                                           scalar=bc[:, 3:4], in1=xcomb[t % 2][:],
                                           op0=ALU.mult, op1=ALU.add)
                    e.drain().then_inc(sxc, 1)
                    if t >= 2:
                        e.wait_ge(sso[t % 2], 16 * ((t - 2) // 2 + 1))
                    for oh in range(2):
                        e.wait_ge(smmO, uo + 1)
                        e.scalar_tensor_tensor(out=xo[t % 2][:, 512 * oh:512 * (oh + 1)],
                                               in0=pb[uo % 4][:], scalar=rco_sb[:, t:t + 1],
                                               in1=vrep[:, 512 * oh:512 * (oh + 1)],
                                               op0=ALU.mult, op1=ALU.add)
                        e.drain()
                        e.tensor_tensor(out=xo[t % 2][:, 512 * oh:512 * (oh + 1)],
                                        in0=xo[t % 2][:, 512 * oh:512 * (oh + 1)],
                                        in1=xcomb[t % 2][:, 512 * oh:512 * (oh + 1)],
                                        op=ALU.add)
                        e.drain().then_inc(sevO, 1)
                        uo += 1

    tp_x1o = tap("x1o", [TOWN, D], F32)
    if tp_x1o is not None:
        st7 = g.sem("p7_tap")
        with nc.Block() as blk:
            @blk.sync
            def _(e):
                e.sem_clear(st7)
                e.dma_start(out=tp_x1o[:, :], in_=x1o_sp[:, :]).then_inc(st7, 16)
                e.wait_ge(st7, 16)


def _p8(nc, g, xn1T_own, x1o_sp, w1b_sp, w2b_sp, tpTr, ev1r, M_sb, E2b,
        b1_in, b2_in, y_out, onerow_f, tap):
    NTC = 4
    with ExitStack() as es:
        p = P(nc, es)
        w1t = p.sb("p8_w1t", [128, 8 * INNER], BF16)
        h_sb = p.sb("p8_h", [128, 32 * 512], BF16)
        w2t = [p.sb(f"p8_w2t{i}", [128, D], BF16) for i in range(2)]
        uT_sb = p.sb("p8_uT", [16, 512], BF16)
        b1_sb = p.sb("p8_b1", [128, 32], F32)
        b2row = p.sb("p8_b2r", [1, D], F32)
        b2rep = p.sb("p8_b2rep", [128, D], F32)
        x1ot = [p.sb(f"p8_x1o{i}", [128, D], F32) for i in range(2)]
        yo = [p.sb(f"p8_yo{i}", [128, D], F32) for i in range(2)]
        po = [p.ps(f"p8_po{i}", [128, 512], F32) for i in range(8)]
        swl = p.sem("p8_swl", sw=True); swt1 = p.sem("p8_swt1")
        swt2 = [p.sem("p8_swt20"), p.sem("p8_swt21")]
        smmH = p.sem("p8_smmH"); sevH = p.sem("p8_sevH")
        smmT = p.sem("p8_smmT"); sevUT = p.sem("p8_sevUT")
        smmB = p.sem("p8_smmB"); sevB = p.sem("p8_sevB")
        smmO = p.sem("p8_smmO"); sevY = p.sem("p8_sevY")
        sx8 = [p.sem("p8_sx80"), p.sem("p8_sx81")]
        ssy = [p.sem("p8_ssy0"), p.sem("p8_ssy1")]
        p.clear_sems()

        with nc.Block() as blk:
            @blk.gpsimd
            def _(e):
                e.dma_start(out=b1_sb[:], in_=b1_in[:, :]).then_inc(swl, 16)
                e.dma_start(out=b2row[:], in_=b2_in[:, :]).then_inc(swl, 16)

            @blk.sync
            def _(e):
                for k in range(8):
                    e.dma_start(out=w1t[:, INNER * k:INNER * (k + 1)],
                                in_=w1b_sp[:, 128 * k:128 * (k + 1)],
                                transpose=True).then_inc(swt1, 16)
                for tc in range(NTC):
                    for j in range(32):
                        jj = 32 * tc + j
                        if jj >= 2:
                            e.wait_ge(smmO, jj - 1)
                        e.dma_start(out=w2t[jj % 2][:], in_=w2b_sp[:, 128 * j:128 * (j + 1)],
                                    transpose=True).then_inc(swt2[jj % 2], 16)
                    for tt in range(4):
                        ti = 4 * tc + tt
                        if ti >= 2:
                            e.wait_ge(sevY, 2 * (ti - 1))
                        e.dma_start(out=x1ot[ti % 2][:],
                                    in_=x1o_sp[128 * ti:128 * (ti + 1), :]).then_inc(sx8[ti % 2], 16)
                    for tt in range(4):
                        ti = 4 * tc + tt
                        e.wait_ge(sevY, 2 * ti + 2)
                        e.dma_start(out=y_out[128 * ti:128 * (ti + 1), :],
                                    in_=yo[ti % 2][:]).then_inc(ssy[ti % 2], 16)
                e.wait_ge(ssy[0], 16 * 8)
                e.wait_ge(ssy[1], 16 * 8)

            @blk.tensor
            def _(e):
                e.wait_ge(swl, 32)
                for oh in range(2):
                    e.matmul(po[5][:, 0:512], onerow_f[0:1, 0:128],
                             b2row[0:1, 512 * oh:512 * (oh + 1)],
                             start=True, stop=True).then_inc(smmB, 1)
                    e.wait_ge(sevB, oh + 1)
                e.wait_ge(swt1, 16 * 8)
                for tc in range(NTC):
                    if tc >= 1:
                        e.wait_ge(sevY, 8 * tc)
                    for m in range(32):
                        hm = 32 * tc + m
                        if hm >= 4:
                            e.wait_ge(sevH, hm - 3)
                        for k in range(8):
                            e.matmul(po[m % 4][:],
                                     w1t[:, INNER * k + 128 * m:INNER * k + 128 * (m + 1)],
                                     xn1T_own[:, TOWN * k + 512 * tc:TOWN * k + 512 * (tc + 1)],
                                     start=(k == 0), stop=False)
                        q = m // 8
                        e.matmul(po[m % 4][:],
                                 ev1r[0:1, 1024 * q + 128 * (m % 8):1024 * q + 128 * (m % 8) + 128],
                                 tpTr[0:1, 2048 * q + 512 * tc:2048 * q + 512 * (tc + 1)],
                                 start=False, stop=True).then_inc(smmH, 1)
                    e.wait_ge(sevH, 32 * (tc + 1))
                    for j in range(32):
                        e.matmul(po[4][0:16, :], M_sb[:, 16 * j:16 * (j + 1)],
                                 h_sb[:, 512 * j:512 * (j + 1)],
                                 start=(j == 0), stop=(j == 31)).then_inc(smmT, 1 if j == 31 else 0)
                    e.wait_ge(sevUT, tc + 1)
                    for tt in range(4):
                        for oh in range(2):
                            e.matmul(po[2 * tt + oh][:], uT_sb[:, 128 * tt:128 * (tt + 1)],
                                     E2b[0:16, 512 * oh:512 * (oh + 1)],
                                     start=True, stop=False)
                    for j in range(32):
                        jj = 32 * tc + j
                        e.wait_ge(swt2[jj % 2], 16 * (jj // 2 + 1))
                        for tt in range(4):
                            for oh in range(2):
                                ins = e.matmul(po[2 * tt + oh][:],
                                               h_sb[:, 512 * j + 128 * tt:512 * j + 128 * (tt + 1)],
                                               w2t[jj % 2][:, 512 * oh:512 * (oh + 1)],
                                               start=False, stop=(j == 31))
                        ins.then_inc(smmO, 1)

            @blk.scalar
            def _(e):
                for tc in range(NTC):
                    for m in range(32):
                        hm = 32 * tc + m
                        e.wait_ge(smmH, hm + 1)
                        e.activation(out=h_sb[:, 512 * m:512 * (m + 1)], in_=po[m % 4][:],
                                     func=AF.Gelu, bias=b1_sb[:, m:m + 1]).then_inc(sevH, 1)

            @blk.vector
            def _(e):
                for oh in range(2):
                    e.wait_ge(smmB, oh + 1)
                    e.tensor_copy(b2rep[:, 512 * oh:512 * (oh + 1)], po[5][:, 0:512])
                    e.drain().then_inc(sevB, 1)
                for tc in range(NTC):
                    e.wait_ge(smmT, tc + 1)
                    e.tensor_copy(uT_sb[:], po[4][0:16, :])
                    e.drain().then_inc(sevUT, 1)
                    e.wait_ge(smmO, 32 * (tc + 1))
                    for tt in range(4):
                        ti = 4 * tc + tt
                        e.wait_ge(sx8[ti % 2], 16 * (ti // 2 + 1))
                        if ti >= 2:
                            e.wait_ge(ssy[ti % 2], 16 * ((ti - 2) // 2 + 1))
                        for oh in range(2):
                            b = 2 * tt + oh
                            e.tensor_tensor(out=yo[ti % 2][:, 512 * oh:512 * (oh + 1)],
                                            in0=po[b][:], in1=b2rep[:, 512 * oh:512 * (oh + 1)],
                                            op=ALU.add)
                            e.drain()
                            e.tensor_tensor(out=yo[ti % 2][:, 512 * oh:512 * (oh + 1)],
                                            in0=yo[ti % 2][:, 512 * oh:512 * (oh + 1)],
                                            in1=x1ot[ti % 2][:, 512 * oh:512 * (oh + 1)],
                                            op=ALU.add)
                            e.drain().then_inc(sevY, 1)



def _mk_inputs(inputs):
    x = np.ascontiguousarray(np.asarray(inputs["L0_attn_x"], dtype=np.float32))
    B = x.shape[0]
    f32 = lambda a: np.ascontiguousarray(np.asarray(a, dtype=np.float32))
    wq = f32(inputs["L0_attn_Wq"]).reshape(D, D)
    wk = f32(inputs["L0_attn_Wk"]).reshape(D, D)
    wv = f32(inputs["L0_attn_Wv"]).reshape(D, D)
    wout = f32(inputs["L0_attn_Wout"]).reshape(D, D)
    common = {
        "Wq": wq, "Wk": wk, "Wv": wv, "Wout": wout,
        "w_gate": f32(inputs["L0_ffn_w_gate"]),
        "w_up": f32(inputs["L0_ffn_w_up"]),
        "w_down": f32(inputs["L0_ffn_w_down"]),
        "WQ1": f32(inputs["L1_attn_WQ"]), "WK1": f32(inputs["L1_attn_WK"]),
        "WO1": f32(inputs["L1_attn_WO"]), "Winv": f32(inputs["L1_attn_Winv"]),
        "U_b": f32(inputs["L1_attn_U_b"]), "V_b": f32(inputs["L1_attn_V_b"]),
        "W_b": f32(inputs["L1_attn_W_b"]), "U_t": f32(inputs["L1_attn_U_t"]),
        "V_t": f32(inputs["L1_attn_V_t"]), "W_t": f32(inputs["L1_attn_W_t"]),
        "X_t": f32(inputs["L1_attn_X_t"]),
        "bias_b": f32(inputs["L1_attn_bias_b"]).reshape(1, D),
        "bias_t": f32(inputs["L1_attn_bias_t"]).reshape(1, D),
        "W1": f32(inputs["L1_ffn_W1"]),
        "b1": f32(inputs["L1_ffn_b1"]).reshape(INNER // 128, 128).T.copy(),
        "W2": f32(inputs["L1_ffn_W2"]),
        "b2": f32(inputs["L1_ffn_b2"]).reshape(1, D),
        "rcf": (1.0 / np.arange(1, T + 1, dtype=np.float64)).astype(np.float32).reshape(1, T),
        "ident": np.eye(128, dtype=np.float32),
    }
    in_maps = []
    for c in range(8):
        b, half = c // 2, c % 2
        hoff = half * TOWN
        scal = np.array([[np.float32(inputs["L1_attn_alpha_bi"]),
                          np.float32(inputs["L1_attn_alpha_tri"]),
                          np.float32(inputs["L1_ffn_scale"]),
                          np.float32(half), np.float32(1 - half), 0.0]], dtype=np.float32)
        counts = np.arange(hoff + 1, hoff + TOWN + 1, dtype=np.float64)
        rco = (1.0 / counts).astype(np.float32).reshape(NOWN, 128).T.copy()
        m = dict(common)
        m["x"] = x[b]
        m["hoffs"] = np.array([[hoff, hoff * D, 0, 0]], dtype=np.int32)
        m["scal"] = scal
        m["rco"] = rco
        in_maps.append(m)
    return in_maps


_PROG_CACHE = {}


def _get_prog(taps=()):
    key = tuple(sorted(taps))
    if key not in _PROG_CACHE:
        _PROG_CACHE[key] = build_program(taps)
    return _PROG_CACHE[key]


def kernel(**inputs):
    nc, _dbg = _get_prog()
    in_maps = _mk_inputs(inputs)
    res = run_bass_kernel_spmd(nc, in_maps, core_ids=list(range(8)))
    x = np.asarray(inputs["L0_attn_x"])
    B = x.shape[0]
    out = np.empty((B, T, D), dtype=np.float32)
    for b in range(B):
        out[b, :TOWN] = res.results[2 * b]["y"]
        out[b, TOWN:] = res.results[2 * b + 1]["y"]
    return out
